# version 14
# baseline (speedup 1.0000x reference)
import numpy as np
import ml_dtypes
from contextlib import ExitStack

import concourse.bass as bass
import concourse.mybir as mybir
from concourse.bass_utils import run_bass_kernel_spmd

F32 = mybir.dt.float32
BF16 = mybir.dt.bfloat16
AF = mybir.ActivationFunctionType
ALU = mybir.AluOpType
AX = mybir.AxisListType

NCORE = 8
B, S, D = 4, 8192, 2048
TOK = B * S
TOKC = TOK // NCORE
NT = TOKC // 128
NBLK = 2
BT = NT // NBLK
NKD = D // 128
HV, HK, DK, DV = 32, 16, 128, 128
KEY, VAL = HK * DK, HV * DV
E8 = 8
RD = 1.0 / D
EPS_LN = 1e-5
EPS_AT = 1e-6
DKI = DK ** (-0.5)

ENGS = ("sync", "gpsimd", "vector", "scalar", "tensor")


class Prog:

    def __init__(self, nc):
        self.nc = nc
        self.ops = {e: [] for e in ENGS}
        self.cnt = {}
        self.hand = {}
        self.mute = False

    def sem(self, ctx, name):
        self.hand[name] = ctx.enter_context(self.nc.semaphore(name))
        self.cnt[name] = 0

    def emit(self, eng, builder, waits=(), incs=()):
        if self.mute:
            return
        rw = [(self.hand[nm], int(v)) for nm, v in waits]
        ri = []
        for nm, n in incs:
            self.cnt[nm] += n
            ri.append((self.hand[nm], n, self.cnt[nm]))
        self.ops[eng].append((builder, rw, ri))

    def v(self, nm):
        return self.cnt[nm]

    def finalize(self, block):
        self_sync = {"vector": "vq", "scalar": "sq"}
        for eng in ENGS:
            ops = self.ops[eng]
            if not ops:
                continue

            def body(e, ops=ops, eng=eng):
                sq = self.hand.get(self_sync.get(eng))
                sq_cnt = 0
                last_edge = None
                for builder, waits, incs in ops:
                    for h, val in waits:
                        if val > 0:
                            e.wait_ge(h, val)
                    if sq is not None and last_edge is not None:
                        e.wait_ge(last_edge[0], last_edge[1])
                    ins = builder(e)
                    for h, n, post in incs:
                        if ins is None:
                            e.sem_inc(h, n)
                        else:
                            ins = ins.then_inc(h, n)
                    if sq is not None:
                        if ins is None:
                            if incs:
                                pass
                        elif incs:
                            last_edge = (incs[0][0], incs[0][2])
                        else:
                            sq_cnt += 1
                            ins.then_inc(sq, 1)
                            last_edge = (sq, sq_cnt)
            getattr(block, eng)(body)


def build_program(use_mini=True, use_ffn=True, use_coll=True, dbg=False):
    mlvl = 7 if use_mini is True else (0 if use_mini is False else int(use_mini))
    nc = bass.Bass("TRN2", target_bir_lowering=False, detect_race_conditions=False)

    x_sl = nc.dram_tensor("x_sl", [TOKC, D], F32, kind="ExternalInput").ap()
    x8 = nc.dram_tensor("x8", [8, D], F32, kind="ExternalInput").ap()
    w1T = nc.dram_tensor("w1T", [D, D], F32, kind="ExternalInput").ap()
    w3T = nc.dram_tensor("w3T", [D, D], F32, kind="ExternalInput").ap()
    w2T = nc.dram_tensor("w2T", [D, D], F32, kind="ExternalInput").ap()
    wrT = nc.dram_tensor("wrT", [D, E8], F32, kind="ExternalInput").ap()
    wqT = nc.dram_tensor("wqT", [D, 256], F32, kind="ExternalInput").ap()
    wkT = nc.dram_tensor("wkT", [D, 256], F32, kind="ExternalInput").ap()
    wvT = nc.dram_tensor("wvT", [D, 512], F32, kind="ExternalInput").ap()
    wzT = nc.dram_tensor("wzT", [D, 512], F32, kind="ExternalInput").ap()
    wabT = nc.dram_tensor("wabT", [D, 8], F32, kind="ExternalInput").ap()
    woT = nc.dram_tensor("woT", [512, D], F32, kind="ExternalInput").ap()
    cwI = nc.dram_tensor("cw", [1024, 4], F32, kind="ExternalInput").ap()
    alogI = nc.dram_tensor("alog", [4, 1], F32, kind="ExternalInput").ap()
    dtbI = nc.dram_tensor("dtb", [4, 1], F32, kind="ExternalInput").ap()
    normwI = nc.dram_tensor("normw", [128, 1], F32, kind="ExternalInput").ap()
    mask8I = nc.dram_tensor("mask8", [8, 1], F32, kind="ExternalInput").ap()
    onesI = nc.dram_tensor("ones", [128, 128], F32, kind="ExternalInput").ap()
    identI = nc.dram_tensor("ident", [128, 128], F32, kind="ExternalInput").ap()
    identbI = nc.dram_tensor("identb", [128, 128], BF16, kind="ExternalInput").ap()
    out = nc.dram_tensor("out", [TOKC, D], F32, kind="ExternalOutput").ap()

    if dbg:
        dbg1 = nc.dram_tensor("dbg1", [8, D], F32, kind="ExternalOutput").ap()
        dbg2 = nc.dram_tensor("dbg2", [8, D], F32, kind="ExternalOutput").ap()
        dbg3 = nc.dram_tensor("dbg3", [16, D], F32, kind="ExternalOutput").ap()
    ccin = nc.dram_tensor("ccin", [16, D], F32)
    ccout = nc.dram_tensor("ccout", [16, D], F32)

    with ExitStack() as root:
        e = root.enter_context
        ones = e(nc.sbuf_tensor("ones_sb", [128, 128], F32))
        ident = e(nc.sbuf_tensor("ident_sb", [128, 128], F32))
        identb = e(nc.sbuf_tensor("identb_sb", [128, 128], BF16))
        s8acc = e(nc.sbuf_tensor("s8acc", [8, D], F32))
        mask8 = e(nc.sbuf_tensor("mask8_sb", [8, 1], F32))

        cst = e(nc.sbuf_tensor("cst", [128, 4], F32))

        p = Prog(nc)
        for s_ in ("ldx", "gpd", "pe", "ve", "se", "st", "cc", "vq", "sq"):
            p.sem(root, s_)

        p.emit("vector", lambda v: v.memset(cst[:, 0:1], EPS_LN))
        p.emit("vector", lambda v: v.memset(cst[:, 1:2], EPS_AT))
        p.emit("vector", lambda v: v.memset(cst[:, 2:3], 1.0), incs=[("ve", 1)])

        p.emit("sync", lambda s: s.dma_start(out=ones[:, :], in_=onesI), incs=[("ldx", 16)])
        p.emit("sync", lambda s: s.dma_start(out=ident[:, :], in_=identI), incs=[("ldx", 16)])
        p.emit("sync", lambda s: s.dma_start(out=identb[:, :], in_=identbI), incs=[("ldx", 16)])
        p.emit("sync", lambda s: s.dma_start(out=mask8[:, :], in_=mask8I), incs=[("ldx", 16)])
        const_ld = p.v("ldx")

        p.mute = mlvl < 1
        with ExitStack() as ma:
            em = ma.enter_context
            wq = em(nc.sbuf_tensor("wq", [128, NKD, 256], F32))
            wk = em(nc.sbuf_tensor("wk", [128, NKD, 256], F32))
            wv = em(nc.sbuf_tensor("wv", [128, NKD, 512], F32))
            wz = em(nc.sbuf_tensor("wz", [128, NKD, 512], F32))
            wab = em(nc.sbuf_tensor("wab", [128, NKD, 8], F32))
            wo = em(nc.sbuf_tensor("wo", [128, 4, D], F32))
            cw = em(nc.sbuf_tensor("cw_sb", [128, 8, 4], F32))
            alog = em(nc.sbuf_tensor("alog_sb", [4, 1], F32))
            dtb = em(nc.sbuf_tensor("dtb_sb", [4, 1], F32))
            normw = em(nc.sbuf_tensor("normw_sb", [128, 1], F32))
            x8s = em(nc.sbuf_tensor("x8s", [8, D], F32))
            xn8 = em(nc.sbuf_tensor("xn8", [128, D], F32))
            st8 = em(nc.sbuf_tensor("st8", [8, 8], F32))
            xn8fm = em(nc.sbuf_tensor("xn8fm", [128, NKD, 8], F32))
            rawp = em(nc.sbuf_tensor("rawp", [128, 8, 11], F32))
            cvt = em(nc.sbuf_tensor("cvt", [128, 8, 8], F32))
            mx = em(nc.sbuf_tensor("mx", [128, 8, 8], F32))
            zt = em(nc.sbuf_tensor("zt", [128, 4, 8], F32))
            absb = em(nc.sbuf_tensor("absb", [4, 16], F32))
            sqs = em(nc.sbuf_tensor("sqs", [128, 8], F32))
            nr = em(nc.sbuf_tensor("nr", [128, 32], F32))
            qn = em(nc.sbuf_tensor("qn", [128, 4, 8], F32))
            a2 = em(nc.sbuf_tensor("a2", [4, 8], F32))
            spx = em(nc.sbuf_tensor("spx", [4, 8], F32))
            spl = em(nc.sbuf_tensor("spl", [4, 8], F32))
            spm = em(nc.sbuf_tensor("spm", [4, 8], F32))
            gd = em(nc.sbuf_tensor("gd", [4, 8], F32))
            Es = em(nc.sbuf_tensor("Es", [4, 8], F32))
            Bs = em(nc.sbuf_tensor("Bs", [128, 8], F32))
            eA = em(nc.sbuf_tensor("eA", [4, 1], F32))
            bT = em(nc.sbuf_tensor("bT", [8, 4], F32))
            erow = em(nc.sbuf_tensor("erow", [128, 32], F32))
            Ebc = em(nc.sbuf_tensor("Ebc", [128, 4, 8], F32))
            ktr = em(nc.sbuf_tensor("ktr", [8, 128], F32))
            vtr = em(nc.sbuf_tensor("vtr", [8, 128], F32))
            kbt = em(nc.sbuf_tensor("kbt", [8, 128], F32))
            krow = em(nc.sbuf_tensor("krow", [128, 1024], F32))
            vrow = em(nc.sbuf_tensor("vrow", [1, 1024], F32))
            dls = em(nc.sbuf_tensor("dls", [128, 128], F32))
            Sst = em(nc.sbuf_tensor("Sst", [128, 128], F32))
            Oall = em(nc.sbuf_tensor("Oall", [128, 4, 8], F32))
            Ofin = em(nc.sbuf_tensor("Ofin", [128, 4, 8], F32))
            szal = em(nc.sbuf_tensor("szal", [128, 4, 8], F32))

            def gld(dst, src):
                p.emit("gpsimd", lambda g, d=dst, s=src: g.dma_start(out=d, in_=s),
                       incs=[("gpd", 16)])
            gld(x8s[:, :], x8)
            gld(cw[:, :, :], cwI.rearrange("(c p) k -> p c k", p=128))
            gld(alog[:, :], alogI)
            gld(dtb[:, :], dtbI)
            gld(normw[:, :], normwI)
            mini_g1 = p.v("gpd")
            gld(wq[:, :, :], wqT.rearrange("(k p) m -> p k m", p=128))
            gld(wk[:, :, :], wkT.rearrange("(k p) m -> p k m", p=128))
            gld(wab[:, :, :], wabT.rearrange("(k p) m -> p k m", p=128))
            mini_g2 = p.v("gpd")
            gld(wv[:, :, :], wvT.rearrange("(k p) m -> p k m", p=128))
            gld(wz[:, :, :], wzT.rearrange("(k p) m -> p k m", p=128))
            mini_g3 = p.v("gpd")
            mini_ld = p.v("gpd")
            p.emit("sync", lambda s: s.dma_start(out=wo[:, :, :],
                                                 in_=woT.rearrange("(h p) m -> p h m", p=128)),
                   incs=[("ldx", 16)])
            ldx_wo = p.v("ldx")

            p.emit("vector", lambda v: v.memset(xn8[:, :], 0.0))
            p.emit("vector", lambda v: v.memset(Bs[:, :], 0.0))
            p.emit("vector", lambda v: v.memset(nr[:, :], 0.0))
            p.emit("vector", lambda v: v.memset(erow[:, :], 0.0))
            p.emit("vector", lambda v: v.memset(krow[:, :], 0.0))
            p.emit("vector", lambda v: v.memset(dls[:, :], 0.0), incs=[("ve", 1)])

            with ExitStack() as mp1:
                e1 = mp1.enter_context
                ps_t8 = e1(nc.psum_tensor("ps_t8", [128, NKD, 8], F32))
                ps_p = e1(nc.psum_tensor("ps_p", [128, 8], F32))
                ps_p2 = e1(nc.psum_tensor("ps_p2", [128, 8], F32))
                ps_ab = e1(nc.psum_tensor("ps_ab", [4, 16], F32))
                ps_n = e1(nc.psum_tensor("ps_n", [1, 64], F32))
                ps_b = e1(nc.psum_tensor("ps_b", [128, 64], F32))
                ps_t2 = e1(nc.psum_tensor("ps_t2", [8, 128], F32))

                p.emit("vector", lambda v: v.reduce_sum(st8[:, 0:1], x8s[:, :], axis=AX.X, negate=True),
                       waits=[("gpd", mini_g1)])
                p.emit("vector", lambda v: v.tensor_scalar_mul(st8[:, 1:2], st8[:, 0:1], RD))
                ve_nm = p.v("ve")
                p.emit("vector", lambda v: None, incs=[("ve", 1)])
                p.emit("scalar", lambda s: s.activation(xn8[0:8, :], x8s[:, :], AF.Square,
                                                        bias=st8[:, 1:2], scale=1.0,
                                                        accum_out=st8[:, 2:3]),
                       waits=[("ve", p.v("ve"))])
                p.emit("scalar", lambda s: s.activation(st8[:, 3:4], st8[:, 2:3], AF.Sqrt,
                                                        bias=cst[0:8, 0:1], scale=RD),
                       incs=[("se", 1)])
                se_sv = p.v("se")
                p.emit("vector", lambda v: v.reciprocal(st8[:, 4:5], st8[:, 3:4]),
                       waits=[("se", se_sv)])
                p.emit("vector", lambda v: v.tensor_scalar(xn8[0:8, :], x8s[:, :], st8[:, 1:2],
                                                           st8[:, 4:5], ALU.add, ALU.mult),
                       incs=[("ve", 1)])
                ve_xn8 = p.v("ve")

                p.mute = mlvl < 2
                for j in range(NKD):
                    p.emit("tensor", lambda t, j=j: t.matmul(ps_t8[:, j, :], xn8[:, j * 128:(j + 1) * 128],
                                                             ident[:, 0:8], start=True, stop=True),
                           waits=[("ve", ve_xn8), ("ldx", const_ld)] if j == 0 else [],
                           incs=[("pe", 1)] if j == NKD - 1 else [])
                pe_t8 = p.v("pe")
                p.emit("vector", lambda v: v.tensor_copy(xn8fm[:, :, :], ps_t8[:, :, :]),
                       waits=[("pe", pe_t8)], incs=[("ve", 1)])
                ve_fm = p.v("ve")

                p.emit("vector", lambda v: v.memset(rawp[:, :, :], 0.0), incs=[("ve", 1)])
                ve_pad = p.v("ve")

                proj = []
                for mo in range(2):
                    proj.append((wq, mo, ("rawp", mo)))
                for mo in range(2):
                    proj.append((wk, mo, ("rawp", 2 + mo)))
                for mo in range(4):
                    proj.append((wv, mo, ("rawp", 4 + mo)))
                for mo in range(4):
                    proj.append((wz, mo, ("zt", mo)))
                pe_evac = {}
                for i, (wt, mo, dst) in enumerate(proj):
                    par = i % 2
                    ps = ps_p if par == 0 else ps_p2
                    w8 = []
                    if i == 0:
                        w8.append(("ve", ve_fm))
                        w8.append(("gpd", mini_g2))
                    if i == 4:
                        w8.append(("gpd", mini_g3))
                    if par in pe_evac:
                        w8.append(("ve", pe_evac[par]))
                    for ki in range(NKD):
                        p.emit("tensor",
                               lambda t, wt=wt, mo=mo, ki=ki, ps=ps: t.matmul(
                                   ps[:, :], wt[:, ki, mo * 128:(mo + 1) * 128], xn8fm[:, ki, :],
                                   start=(ki == 0), stop=(ki == NKD - 1)),
                               waits=w8 if ki == 0 else [],
                               incs=[("pe", 1)] if ki == NKD - 1 else [])
                    pe_mm = p.v("pe")
                    kind, ch = dst
                    if kind == "rawp":
                        p.emit("vector", lambda v, ch=ch, ps=ps: v.tensor_copy(rawp[:, ch, 3:11], ps[:, :]),
                               waits=[("pe", pe_mm), ("ve", ve_pad)], incs=[("ve", 1)])
                    else:
                        p.emit("vector", lambda v, ch=ch, ps=ps: v.tensor_copy(zt[:, ch, :], ps[:, :]),
                               waits=[("pe", pe_mm)], incs=[("ve", 1)])
                    pe_evac[par] = p.v("ve")
                for half in range(2):
                    for ki in range(NKD):
                        p.emit("tensor",
                               lambda t, ki=ki, half=half: t.matmul(
                                   ps_ab[:, 8 * half:8 * half + 8],
                                   wab[:, ki, 4 * half:4 * half + 4], xn8fm[:, ki, :],
                                   start=(ki == 0), stop=(ki == NKD - 1)),
                               waits=[("gpd", mini_g2)] if (half == 0 and ki == 0) else [],
                               incs=[("pe", 1)] if ki == NKD - 1 else [])
                pe_ab = p.v("pe")
                p.emit("vector", lambda v: v.tensor_copy(absb[:, :], ps_ab[:, :]),
                       waits=[("pe", pe_ab)], incs=[("ve", 1)])
                ve_ab = p.v("ve")

                p.mute = mlvl < 3
                for ch in range(8):
                    p.emit("vector", lambda v, ch=ch: v.tensor_scalar(
                        cvt[:, ch, :], rawp[:, ch, 0:8], cw[:, ch, 0:1], None, ALU.mult))
                    for k in range(1, 4):
                        p.emit("vector", lambda v, ch=ch, k=k: v.scalar_tensor_tensor(
                            cvt[:, ch, :], rawp[:, ch, k:k + 8], cw[:, ch, k:k + 1],
                            cvt[:, ch, :], ALU.mult, ALU.add))
                p.emit("vector", lambda v: None, incs=[("ve", 1)])
                ve_cvt = p.v("ve")
                p.emit("scalar", lambda s: s.activation(mx[:, :, :], cvt[:, :, :], AF.Silu),
                       waits=[("ve", ve_cvt)], incs=[("se", 1)])
                se_mx = p.v("se")

                for ch in range(4):
                    p.emit("scalar", lambda s, ch=ch: s.activation(sqs[:, :], mx[:, ch, :], AF.Square),
                           waits=[("se", se_mx)], incs=[("se", 1)])
                    se_sq = p.v("se")
                    p.emit("tensor", lambda t, ch=ch: t.matmul(ps_n[0:1, 8 * ch:8 * ch + 8],
                                                               ones[:, 0:1], sqs[:, :],
                                                               start=True, stop=True),
                           waits=[("se", se_sq)], incs=[("pe", 1)])
                    pe_n = p.v("pe")
                    p.emit("scalar", lambda s, ch=ch: s.activation(nr[0:1, 8 * ch:8 * ch + 8],
                                                                   ps_n[0:1, 8 * ch:8 * ch + 8], AF.Sqrt),
                           waits=[("pe", pe_n)], incs=[("se", 1)])
                    se_nr = p.v("se")
                    p.emit("vector", lambda v, ch=ch: v.tensor_scalar_max(
                        nr[0:1, 8 * ch:8 * ch + 8], nr[0:1, 8 * ch:8 * ch + 8], 1e-6),
                           waits=[("se", se_nr)])
                    p.emit("vector", lambda v, ch=ch: v.reciprocal(
                        nr[0:1, 8 * ch:8 * ch + 8], nr[0:1, 8 * ch:8 * ch + 8]))
                    if ch < 2:
                        p.emit("vector", lambda v, ch=ch: v.tensor_scalar_mul(
                            nr[0:1, 8 * ch:8 * ch + 8], nr[0:1, 8 * ch:8 * ch + 8], DKI))
                    p.emit("vector", lambda v: None, incs=[("ve", 1)])
                    ve_r = p.v("ve")
                    p.emit("tensor", lambda t, ch=ch: t.matmul(ps_b[:, 8 * ch:8 * ch + 8],
                                                               ones[:, :], nr[:, 8 * ch:8 * ch + 8],
                                                               start=True, stop=True),
                           waits=[("ve", ve_r)], incs=[("pe", 1)])
                    pe_b = p.v("pe")
                    p.emit("vector", lambda v, ch=ch: v.tensor_mul(qn[:, ch, :], mx[:, ch, :],
                                                                   ps_b[:, 8 * ch:8 * ch + 8]),
                           waits=[("pe", pe_b)], incs=[("ve", 1)])
                ve_qn = p.v("ve")

                p.mute = mlvl < 4
                p.emit("vector", lambda v: v.tensor_scalar_add(a2[:, :], absb[0:4, 0:8], dtb[:, :]),
                       waits=[("ve", ve_ab)], incs=[("ve", 1)])
                ve_a2 = p.v("ve")
                p.emit("scalar", lambda s: s.activation(spx[:, :], a2[:, :], AF.Abs),
                       waits=[("ve", ve_a2)])
                p.emit("scalar", lambda s: s.activation(spx[:, :], spx[:, :], AF.Exp, scale=-1.0))
                p.emit("scalar", lambda s: s.activation(spl[:, :], spx[:, :], AF.Ln, bias=cst[0:4, 2:3]),
                       incs=[("se", 1)])
                se_spl = p.v("se")
                p.emit("vector", lambda v: v.tensor_scalar_max(spm[:, :], a2[:, :], 0.0))
                p.emit("vector", lambda v: v.tensor_add(spm[:, :], spm[:, :], spl[:, :]),
                       waits=[("se", se_spl)], incs=[("ve", 1)])
                ve_sp = p.v("se")
                p.emit("scalar", lambda s: s.activation(eA[:, :], alog[:, :], AF.Exp),
                       incs=[("se", 1)])
                se_eA = p.v("se")
                p.emit("vector", lambda v: v.tensor_scalar_mul(eA[:, :], eA[:, :], -1.0),
                       waits=[("se", se_eA)])
                p.emit("vector", lambda v: v.tensor_scalar(gd[:, :], spm[:, :], eA[:, :], None, ALU.mult),
                       incs=[("ve", 1)])
                ve_gd = p.v("ve")
                p.emit("scalar", lambda s: s.activation(Es[:, :], gd[:, :], AF.Exp),
                       waits=[("ve", ve_gd)])
                p.emit("scalar", lambda s: s.activation(Bs[0:4, :], absb[0:4, 8:16], AF.Sigmoid),
                       incs=[("se", 1)])
                se_EB = p.v("se")

                p.emit("tensor", lambda t: t.matmul(ps_t2[0:8, 0:4], Bs[:, :], ident[:, 0:4],
                                                    start=True, stop=True),
                       waits=[("se", se_EB)], incs=[("pe", 1)])
                pe_bT = p.v("pe")
                p.emit("vector", lambda v: v.tensor_copy(bT[:, :], ps_t2[0:8, 0:4]),
                       waits=[("pe", pe_bT)], incs=[("ve", 1)])
                ve_bT = p.v("ve")
                p.emit("gpsimd", lambda g: g.dma_start(out=erow[0:1, :], in_=Es[:, :]),
                       waits=[("se", se_EB)], incs=[("gpd", 16)])
                gp_er = p.v("gpd")

                ve_ebc = 0
                for h in range(4):
                    p.emit("tensor", lambda t, h=h: t.matmul(ps_b[:, 8 * h:8 * h + 8], ones[:, :],
                                                             erow[:, 8 * h:8 * h + 8],
                                                             start=True, stop=True),
                           waits=([("gpd", gp_er), ("ve", ve_qn)] if h == 0
                                  else [("ve", ve_ebc)]),
                           incs=[("pe", 1)])
                    pe_eb = p.v("pe")
                    p.emit("vector", lambda v, h=h: v.tensor_copy(Ebc[:, h, :], ps_b[:, 8 * h:8 * h + 8]),
                           waits=[("pe", pe_eb)], incs=[("ve", 1)])
                    ve_ebc = p.v("ve")

            with ExitStack() as mp2:
                e2 = mp2.enter_context
                ps_t2b = e2(nc.psum_tensor("ps_t2b", [8, 128], F32))
                ps_kv = e2(nc.psum_tensor("ps_kv", [1, 128], F32))
                ps_ou = e2(nc.psum_tensor("ps_ou", [128, 128], F32))
                ps_oc = e2(nc.psum_tensor("ps_oc", [128, 1], F32))
                ps_n2 = e2(nc.psum_tensor("ps_n2", [1, 32], F32))
                ps_b2 = e2(nc.psum_tensor("ps_b2", [128, 32], F32))
                ps_s8 = e2(nc.psum_tensor("ps_s8", [8, 512], F32))

                p.mute = mlvl < 5
                pe_head_end = 0
                ve_dl = 0
                ve_vt = ve_bT
                for h in range(4):
                    kh = h // 2
                    p.emit("tensor", lambda t, kh=kh: t.matmul(ps_t2b[:, :], qn[:, 2 + kh, :],
                                                               ident[:, :], start=True, stop=True),
                           waits=[("ve", ve_bT), ("ve", ve_vt)], incs=[("pe", 1)])
                    pe_kt = p.v("pe")
                    p.emit("vector", lambda v: v.tensor_copy(ktr[:, :], ps_t2b[:, :]),
                           waits=[("pe", pe_kt), ("gpd", p.v("gpd"))], incs=[("ve", 1)])
                    ve_kt = p.v("ve")
                    p.emit("vector", lambda v, h=h: v.tensor_scalar(kbt[:, :], ktr[:, :],
                                                                    bT[:, h:h + 1], None, ALU.mult),
                           incs=[("ve", 1)])
                    ve_kb = p.v("ve")
                    p.emit("gpsimd", lambda g: g.dma_start(out=krow[0:1, :], in_=kbt[:, :]),
                           waits=[("ve", ve_kb), ("pe", pe_head_end)], incs=[("gpd", 16)])
                    p.emit("tensor", lambda t, h=h: t.matmul(ps_t2b[:, :], mx[:, 4 + h, :],
                                                             ident[:, :], start=True, stop=True),
                           waits=[("ve", ve_kt)], incs=[("pe", 1)])
                    pe_vt = p.v("pe")
                    p.emit("vector", lambda v: v.tensor_copy(vtr[:, :], ps_t2b[:, :]),
                           waits=[("pe", pe_vt)], incs=[("ve", 1)])
                    ve_vt = p.v("ve")
                    p.emit("gpsimd", lambda g: g.dma_start(out=vrow[:, :], in_=vtr[:, :]),
                           waits=[("ve", ve_vt), ("ve", ve_dl)], incs=[("gpd", 16)])
                    gp_head = p.v("gpd")
                    p.emit("vector", lambda v: v.memset(Sst[:, :], 0.0),
                           waits=[("gpd", gp_head), ("pe", pe_head_end)], incs=[("ve", 1)])
                    for t_ in range(8):
                        ve_S = p.v("ve")
                        p.emit("vector", lambda v, h=h, t_=t_: v.tensor_scalar_mul(
                            Sst[:, :], Sst[:, :], Ebc[:, h, t_:t_ + 1]), incs=[("ve", 1)])
                        ve_dec = p.v("ve")
                        p.emit("tensor", lambda t, kh=kh, t_=t_: t.matmul(
                            ps_kv[:, :], qn[:, 2 + kh, t_:t_ + 1], Sst[:, :], start=True, stop=True),
                               waits=[("ve", ve_dec)], incs=[("pe", 1)])
                        pe_kv = p.v("pe")
                        p.emit("vector", lambda v, t_=t_: v.scalar_tensor_tensor(
                            dls[0:1, :], ps_kv[:, :], -1.0, vrow[0:1, 128 * t_:128 * (t_ + 1)],
                            ALU.mult, ALU.add),
                               waits=[("pe", pe_kv)], incs=[("ve", 1)])
                        ve_dl = p.v("ve")
                        p.emit("tensor", lambda t, t_=t_: t.matmul(
                            ps_ou[:, :], krow[:, 128 * t_:128 * (t_ + 1)],
                            dls[:, :], start=True, stop=True),
                               waits=[("ve", ve_dl)], incs=[("pe", 1)])
                        pe_ou = p.v("pe")
                        p.emit("vector", lambda v: v.tensor_add(Sst[:, :], Sst[:, :], ps_ou[:, :]),
                               waits=[("pe", pe_ou)], incs=[("ve", 1)])
                        ve_Su = p.v("ve")
                        p.emit("tensor", lambda t, kh=kh, t_=t_: t.matmul(
                            ps_oc[:, :], Sst[:, :], qn[:, kh, t_:t_ + 1], start=True, stop=True),
                               waits=[("ve", ve_Su)], incs=[("pe", 1)])
                        pe_oc = p.v("pe")
                        p.emit("vector", lambda v, h=h, t_=t_: v.tensor_copy(
                            Oall[:, h, t_:t_ + 1], ps_oc[:, :]),
                               waits=[("pe", pe_oc)], incs=[("ve", 1)])
                    pe_head_end = p.v("pe")
                ve_rec = p.v("ve")

                p.mute = mlvl < 6
                p.emit("scalar", lambda s: s.activation(szal[:, :, :], zt[:, :, :], AF.Silu),
                       waits=[("ve", pe_evac.get(0, 0)), ("ve", pe_evac.get(1, 0))], incs=[("se", 1)])
                for h in range(4):
                    p.emit("scalar", lambda s, h=h: s.activation(sqs[:, :], Oall[:, h, :], AF.Square),
                           waits=[("ve", ve_rec)], incs=[("se", 1)])
                    se_sq2 = p.v("se")
                    p.emit("tensor", lambda t, h=h: t.matmul(ps_n2[0:1, 8 * h:8 * h + 8], ones[:, 0:1],
                                                             sqs[:, :], start=True, stop=True),
                           waits=[("se", se_sq2)], incs=[("pe", 1)])
                    pe_n2 = p.v("pe")
                    p.emit("scalar", lambda s, h=h: s.activation(nr[0:1, 8 * h:8 * h + 8],
                                                                 ps_n2[0:1, 8 * h:8 * h + 8], AF.Sqrt,
                                                                 scale=1.0 / DV, bias=cst[0:1, 1:2]),
                           waits=[("pe", pe_n2)], incs=[("se", 1)])
                    se_nr2 = p.v("se")
                    p.emit("vector", lambda v, h=h: v.reciprocal(nr[0:1, 8 * h:8 * h + 8],
                                                                 nr[0:1, 8 * h:8 * h + 8]),
                           waits=[("se", se_nr2)], incs=[("ve", 1)])
                    ve_r2 = p.v("ve")
                    p.emit("tensor", lambda t, h=h: t.matmul(ps_b2[:, 8 * h:8 * h + 8], ones[:, :],
                                                             nr[:, 8 * h:8 * h + 8], start=True, stop=True),
                           waits=[("ve", ve_r2)], incs=[("pe", 1)])
                    pe_b2 = p.v("pe")
                    p.emit("vector", lambda v, h=h: v.scalar_tensor_tensor(
                        Ofin[:, h, :], Oall[:, h, :], normw[:, :], ps_b2[:, 8 * h:8 * h + 8],
                        ALU.mult, ALU.mult),
                           waits=[("pe", pe_b2)])
                    p.emit("vector", lambda v, h=h: v.tensor_mul(Ofin[:, h, :], Ofin[:, h, :],
                                                                 szal[:, h, :]),
                           waits=[("se", p.v("se"))], incs=[("ve", 1)])
                ve_of = p.v("ve")

                p.mute = mlvl < 7
                ve_s8ev = 0
                for fo in range(4):
                    for h in range(4):
                        w8_ = []
                        if h == 0:
                            w8_.append(("ve", ve_of if fo == 0 else ve_s8ev))
                            if fo == 0:
                                w8_.append(("ldx", ldx_wo))
                        p.emit("tensor", lambda t, fo=fo, h=h: t.matmul(
                            ps_s8[:, :], Ofin[:, h, :], wo[:, h, fo * 512:(fo + 1) * 512],
                            start=(h == 0), stop=(h == 3)),
                               waits=w8_,
                               incs=[("pe", 1)] if h == 3 else [])
                    pe_s8 = p.v("pe")
                    p.emit("vector", lambda v, fo=fo: v.tensor_copy(s8acc[:, fo * 512:(fo + 1) * 512],
                                                                    ps_s8[:, :]),
                           waits=[("pe", pe_s8)], incs=[("ve", 1)])
                    ve_s8ev = p.v("ve")
                ve_s8 = p.v("ve")

        p.mute = False
        mini_end = {"ve": p.v("ve"), "se": p.v("se"), "pe": p.v("pe"),
                    "gpd": p.v("gpd")}
        if mlvl < 7:
            p.emit("vector", lambda v: v.memset(s8acc[:, :], 0.0), incs=[("ve", 1)])
            ve_s8 = p.v("ve")

        with ExitStack() as ff:
            ef = ff.enter_context
            xin = ef(nc.sbuf_tensor("xin", [128, 2, D], F32))
            xntm = ef(nc.sbuf_tensor("xntm", [128, 2, D], BF16))
            bufX = ef(nc.sbuf_tensor("bufX", [128, NKD, 2048], BF16))
            bufH = ef(nc.sbuf_tensor("bufH", [128, NKD, 2048], BF16))
            wbuf = ef(nc.sbuf_tensor("wbuf", [128, 2, NKD, 512], BF16))
            ybuf = ef(nc.sbuf_tensor("ybuf", [128, 3, 512], BF16))
            rts = ef(nc.sbuf_tensor("rts", [128, 512], F32))
            rtf32 = ef(nc.sbuf_tensor("rtf32", [128, BT, 8], BF16))
            wgtb = ef(nc.sbuf_tensor("wgtb", [128, 2, BT, 8], BF16))
            wrs = ef(nc.sbuf_tensor("wrs", [128, NKD, 8], BF16))
            Pacc = ef(nc.sbuf_tensor("Pacc", [8, D], F32))
            scatacc = Pacc
            PTb = ef(nc.sbuf_tensor("PTb", [128, NKD, 8], BF16))
            nmean = ef(nc.sbuf_tensor("nmean", [128, 2], F32))
            vsum = ef(nc.sbuf_tensor("vsum", [128, 2], F32))
            svt = ef(nc.sbuf_tensor("svt", [128, 2], F32))
            rstd = ef(nc.sbuf_tensor("rstd", [128, 2], F32))
            Et = ef(nc.sbuf_tensor("Et", [128, 8], F32))
            E2t = ef(nc.sbuf_tensor("E2t", [128, 8], F32))
            msk1 = ef(nc.sbuf_tensor("msk1", [128, 8], F32))
            msk2 = ef(nc.sbuf_tensor("msk2", [128, 8], F32))
            rsc = ef(nc.sbuf_tensor("rsc", [128, 8], F32))

            ps_h = ef(nc.psum_tensor("ps_h", [128, 2, 512], F32))
            ps_trA = ef(nc.psum_tensor("ps_trA", [128, 512], BF16))
            ps_trB = ef(nc.psum_tensor("ps_trB", [128, 512], BF16))
            ps_rt = ef(nc.psum_tensor("ps_rt", [8, 512], F32))
            ps_rtt = ef(nc.psum_tensor("ps_rtt", [128, 4, 8], F32))
            ps_P = ef(nc.psum_tensor("ps_P", [8, 512], F32))

            p.mute = not use_ffn
            for r0, r1 in ((8, 1024), (1024, 2048), (2048, 3072), (3072, 4096)):
                p.emit("sync", lambda s, r0=r0, r1=r1: s.dma_start(
                    out=out[r0:r1, :], in_=x_sl[r0:r1, :]),
                       incs=[("st", 16)])
            p.emit("gpsimd", lambda g: g.dma_start(out=wrs[:, :, :],
                                                   in_=wrT.rearrange("(k p) m -> p k m", p=128)),
                   waits=[("ve", mini_end["ve"]), ("se", mini_end["se"]),
                          ("pe", mini_end["pe"])],
                   incs=[("gpd", 16)])
            gp_wr = p.v("gpd")
            p.emit("vector", lambda v: v.memset(rts[:, :], 0.0),
                   waits=[("se", mini_end["se"]), ("pe", mini_end["pe"])])
            p.emit("vector", lambda v: v.memset(Pacc[:, :], 0.0), incs=[("ve", 1)])
            zpad_ve = p.v("ve")

            S = {
                "xin_rd_ve": [0, 0], "xin_rd_se": [0, 0],
                "xntm_rd_pe": [0, 0], "stat_rd_se": [0, 0],
                "ps_h_rd": [0, 0], "ps_h_rd_sem": ["se", "se"],
                "wbuf_rd_pe": [0, 0],
                "bufH_rd_pe": 0, "bufH_wr_se": 0,
                "ps_trX_rd": [("ve", 0), ("ve", 0)],
                "rt_ps_free_ve": zpad_ve, "rts_rd_pe": 0, "rtt_evac_ve": 0,
                "psP_free_ve": zpad_ve,
                "yb_se": [0, 0, 0], "ybuf_rd_pe": [0, 0, 0],
                "wgtb_wr_ve": 0, "ln_done": [0, 0, 0],
            }

            def emit_prep(b):
                base = b * BT
                for tt in range(BT):
                    gt = base + tt
                    par = tt % 2
                    rows = (gt * 128, (gt + 1) * 128)
                    lw = [("ve", S["xin_rd_ve"][par]),
                          ("se", S["xin_rd_se"][par])]
                    if b == 0 and tt < 2:
                        lw += [("ve", mini_end["ve"]), ("se", mini_end["se"]),
                               ("pe", mini_end["pe"]), ("gpd", mini_end["gpd"])]
                    p.emit("sync", lambda s, par=par, rows=rows: s.dma_start(
                        out=xin[:, par, :], in_=x_sl[rows[0]:rows[1], :]),
                           waits=lw, incs=[("ldx", 16)])
                    ld_t = p.v("ldx")
                    p.emit("vector", lambda v, par=par: v.reduce_sum(
                        nmean[:, par:par + 1], xin[:, par, :], axis=AX.X, negate=True),
                           waits=[("ldx", ld_t), ("se", S["stat_rd_se"][par])])
                    p.emit("vector", lambda v, par=par: v.tensor_scalar_mul(
                        nmean[:, par:par + 1], nmean[:, par:par + 1], RD), incs=[("ve", 1)])
                    ve_nm = p.v("ve")
                    p.emit("scalar", lambda s, par=par: s.activation(
                        xntm[:, par, :], xin[:, par, :], AF.Square, bias=nmean[:, par:par + 1],
                        accum_out=vsum[:, par:par + 1]),
                           waits=[("ve", ve_nm), ("pe", S["xntm_rd_pe"][par])])
                    p.emit("scalar", lambda s, par=par: s.activation(
                        svt[:, par:par + 1], vsum[:, par:par + 1], AF.Sqrt, scale=RD, bias=cst[:, 0:1]),
                           incs=[("se", 1)])
                    se_sv = p.v("se")
                    S["xin_rd_se"][par] = se_sv
                    S["stat_rd_se"][par] = se_sv
                    p.emit("vector", lambda v, par=par: v.reciprocal(
                        rstd[:, par:par + 1], svt[:, par:par + 1]),
                           waits=[("se", se_sv)])
                    p.emit("vector", lambda v, par=par: v.tensor_scalar(
                        xntm[:, par, :], xin[:, par, :], nmean[:, par:par + 1],
                        rstd[:, par:par + 1], ALU.add, ALU.mult),
                           incs=[("ve", 1)])
                    ve_norm = p.v("ve")
                    S["xin_rd_ve"][par] = ve_norm
                    for gidx in range(4):
                        bank = ps_trA if gidx % 2 == 0 else ps_trB
                        bwait = S["ps_trX_rd"][gidx % 2]
                        for j4 in range(4):
                            j = gidx * 4 + j4
                            p.emit("tensor", lambda t, par=par, j=j, j4=j4, bank=bank: t.transpose(
                                bank[:, j4 * 128:(j4 + 1) * 128], xntm[:, par, j * 128:(j + 1) * 128],
                                identb[:, :]),
                                   waits=([("ve", ve_norm), (bwait[0], bwait[1])] if j4 == 0 else []),
                                   incs=[("pe", 1)] if j4 == 3 else [])
                        pe_tr = p.v("pe")
                        eng = "vector" if gidx % 2 == 0 else "scalar"
                        sem = "ve" if gidx % 2 == 0 else "se"
                        if eng == "vector":
                            p.emit(eng, lambda ee, tt=tt, gidx=gidx, bank=bank: ee.tensor_copy(
                                bufX[:, gidx * 4:(gidx + 1) * 4, tt * 128:(tt + 1) * 128],
                                bank[:, :].rearrange("p (j t) -> p j t", j=4)),
                                   waits=[("pe", pe_tr)], incs=[(sem, 1)])
                        else:
                            p.emit(eng, lambda ee, tt=tt, gidx=gidx, bank=bank: ee.copy(
                                bufX[:, gidx * 4:(gidx + 1) * 4, tt * 128:(tt + 1) * 128],
                                bank[:, :].rearrange("p (j t) -> p j t", j=4)),
                                   waits=[("pe", pe_tr)], incs=[(sem, 1)])
                        S["ps_trX_rd"][gidx % 2] = (sem, p.v(sem))
                    S["xntm_rd_pe"][par] = p.v("pe")
                    yield
                S["ln_done"] = [p.v("ve"), p.v("se"), p.v("pe")]
                ln_done = S["ln_done"]
                for grp in range(4):
                    for ki in range(NKD):
                        p.emit("tensor", lambda t, grp=grp, ki=ki: t.matmul(
                            ps_rt[:, :], wrs[:, ki, :], bufX[:, ki, grp * 512:(grp + 1) * 512],
                            start=(ki == 0), stop=(ki == NKD - 1)),
                               waits=([("ve", ln_done[0]), ("se", ln_done[1]), ("gpd", gp_wr),
                                       ("ve", S["rt_ps_free_ve"])] if ki == 0 else []),
                               incs=[("pe", 1)] if ki == NKD - 1 else [])
                    pe_rt = p.v("pe")
                    p.emit("vector", lambda v: v.tensor_copy(rts[0:8, :], ps_rt[:, :]),
                           waits=[("pe", pe_rt), ("pe", S["rts_rd_pe"])], incs=[("ve", 1)])
                    ve_rts = p.v("ve")
                    S["rt_ps_free_ve"] = ve_rts
                    for t4 in range(4):
                        p.emit("tensor", lambda t, t4=t4: t.matmul(
                            ps_rtt[:, t4, :], rts[:, t4 * 128:(t4 + 1) * 128], ident[:, 0:8],
                            start=True, stop=True),
                               waits=([("ve", ve_rts), ("ve", S["rtt_evac_ve"])] if t4 == 0 else []),
                               incs=[("pe", 1)] if t4 == 3 else [])
                    pe_tt = p.v("pe")
                    S["rts_rd_pe"] = pe_tt
                    p.emit("vector", lambda v, grp=grp: v.tensor_copy(
                        rtf32[:, grp * 4:(grp + 1) * 4, :], ps_rtt[:, :, :]),
                           waits=[("pe", pe_tt)], incs=[("ve", 1)])
                    S["rtt_evac_ve"] = p.v("ve")
                    yield
                wp = b % 2
                for tt in range(BT):
                    p.emit("vector", lambda v, tt=tt: v.reduce_max(
                        rsc[:, 0:1], rtf32[:, tt, :], axis=AX.X, negate=True), incs=[("ve", 1)])
                    ve_m = p.v("ve")
                    p.emit("scalar", lambda s, tt=tt: s.activation(Et[:, :], rtf32[:, tt, :], AF.Exp,
                                                                   bias=rsc[:, 0:1]),
                           waits=[("ve", ve_m)], incs=[("se", 1)])
                    se_e = p.v("se")
                    p.emit("vector", lambda v: v.reduce_max(rsc[:, 1:2], Et[:, :], axis=AX.X),
                           waits=[("se", se_e)])
                    p.emit("vector", lambda v: v.tensor_scalar(msk1[:, :], Et[:, :], rsc[:, 1:2],
                                                               None, ALU.is_ge))
                    p.emit("vector", lambda v: v.scalar_tensor_tensor(E2t[:, :], msk1[:, :], -1e30,
                                                                      Et[:, :], ALU.mult, ALU.add))
                    p.emit("vector", lambda v: v.reduce_max(rsc[:, 2:3], E2t[:, :], axis=AX.X))
                    p.emit("vector", lambda v: v.tensor_scalar(msk2[:, :], E2t[:, :], rsc[:, 2:3],
                                                               None, ALU.is_ge))
                    p.emit("vector", lambda v: v.tensor_add(rsc[:, 3:4], rsc[:, 1:2], rsc[:, 2:3]))
                    p.emit("vector", lambda v: v.reciprocal(rsc[:, 3:4], rsc[:, 3:4]))
                    p.emit("vector", lambda v: v.tensor_mul(rsc[:, 4:5], rsc[:, 1:2], rsc[:, 3:4]))
                    p.emit("vector", lambda v: v.tensor_mul(rsc[:, 5:6], rsc[:, 2:3], rsc[:, 3:4]))
                    p.emit("vector", lambda v: v.tensor_scalar(E2t[:, :], msk1[:, :], rsc[:, 4:5],
                                                               None, ALU.mult))
                    p.emit("vector", lambda v, tt=tt, wp=wp: v.scalar_tensor_tensor(
                        wgtb[:, wp, tt, :], msk2[:, :], rsc[:, 5:6], E2t[:, :], ALU.mult, ALU.add),
                           incs=[("ve", 1)])
                    yield
                S["wgtb_wr_ve"] = p.v("ve")

            def emit_h1(b):
                ln_done = S["ln_done_h1"]
                for q in range(4):
                    wp = q % 2
                    p.emit("gpsimd", lambda g, q=q, wp=wp: g.dma_start(
                        out=wbuf[:, wp, :, :],
                        in_=w1T[:, q * 512:(q + 1) * 512].rearrange("(k p) m -> p k m", p=128)),
                           waits=[("pe", S["wbuf_rd_pe"][wp])], incs=[("gpd", 16)])
                    gp_w = p.v("gpd")
                    for mo in range(4):
                        fo = 4 * q + mo
                        for n4 in range(4):
                            hp = (mo * 4 + n4) % 2
                            w0 = [(S["ps_h_rd_sem"][hp], S["ps_h_rd"][hp])]
                            if mo == 0 and n4 == 0:
                                w0.append(("gpd", gp_w))
                                if q == 0:
                                    w0 += [("ve", ln_done[0]), ("se", ln_done[1])]
                            for ki in range(NKD):
                                p.emit("tensor", lambda t, wp=wp, ki=ki, mo=mo, n4=n4, hp=hp: t.matmul(
                                    ps_h[:, hp, :],
                                    wbuf[:, wp, ki, mo * 128:(mo + 1) * 128],
                                    bufX[:, ki, n4 * 512:(n4 + 1) * 512],
                                    start=(ki == 0), stop=(ki == NKD - 1)),
                                       waits=w0 if ki == 0 else [],
                                       incs=[("pe", 1)] if ki == NKD - 1 else [])
                            pe_mm = p.v("pe")
                            S["wbuf_rd_pe"][wp] = pe_mm
                            p.emit("scalar", lambda s, fo=fo, n4=n4, hp=hp: s.activation(
                                bufH[:, fo, n4 * 512:(n4 + 1) * 512],
                                ps_h[:, hp, :], AF.Silu),
                                   waits=[("pe", pe_mm), ("pe", S["bufH_rd_pe"])], incs=[("se", 1)])
                            S["ps_h_rd"][hp] = p.v("se")
                            S["ps_h_rd_sem"][hp] = "se"
                S["bufH_wr_se"] = p.v("se")

            def emit_h2(b, gen=None):
                wpb = b % 2
                wgtb_ve_b = S["wgtb_wr_ve"]
                for q in range(4):
                    wp = q % 2
                    p.emit("gpsimd", lambda g, q=q, wp=wp: g.dma_start(
                        out=wbuf[:, wp, :, :],
                        in_=w3T[:, q * 512:(q + 1) * 512].rearrange("(k p) m -> p k m", p=128)),
                           waits=[("pe", S["wbuf_rd_pe"][wp])], incs=[("gpd", 16)])
                    gp_w = p.v("gpd")

                    def emit_pmm(t_, q=q):
                        w = [("se", S["yb_se"][t_ % 3])]
                        if t_ == 0:
                            w += [("ve", S["psP_free_ve"]), ("ve", wgtb_ve_b)]
                        p.emit("tensor", lambda t, t_=t_: t.matmul(
                            ps_P[:, :], wgtb[:, wpb, t_, :], ybuf[:, t_ % 3, :],
                            start=(t_ == 0), stop=(t_ == BT - 1)),
                               waits=w, incs=[("pe", 1)])
                        S["ybuf_rd_pe"][t_ % 3] = p.v("pe")

                    for tt in range(BT):
                        hp = tt % 2
                        w0 = [(S["ps_h_rd_sem"][hp], S["ps_h_rd"][hp])]
                        if tt == 0:
                            w0.append(("gpd", gp_w))
                            if q == 0:
                                w0.append(("se", S["bufH_wr_se"]))
                        for ki in range(NKD):
                            p.emit("tensor", lambda t, wp=wp, ki=ki, tt=tt, hp=hp: t.matmul(
                                ps_h[:, hp, :], bufH[:, ki, tt * 128:(tt + 1) * 128],
                                wbuf[:, wp, ki, :],
                                start=(ki == 0), stop=(ki == NKD - 1)),
                                   waits=w0 if ki == 0 else [],
                                   incs=[("pe", 1)] if ki == NKD - 1 else [])
                        pe_mm = p.v("pe")
                        S["wbuf_rd_pe"][wp] = pe_mm
                        S["bufH_rd_pe"] = pe_mm
                        p.emit("scalar", lambda s, tt=tt, hp=hp: s.activation(
                            ybuf[:, tt % 3, :], ps_h[:, hp, :], AF.Silu),
                               waits=[("pe", pe_mm), ("pe", S["ybuf_rd_pe"][tt % 3])], incs=[("se", 1)])
                        S["ps_h_rd"][hp] = p.v("se")
                        S["ps_h_rd_sem"][hp] = "se"
                        S["yb_se"][tt % 3] = p.v("se")
                        if gen is not None:
                            next(gen, None)
                        if tt >= 2:
                            emit_pmm(tt - 2)
                    emit_pmm(BT - 2)
                    emit_pmm(BT - 1)
                    pe_pl = p.v("pe")
                    p.emit("vector", lambda v, q=q: v.tensor_add(
                        Pacc[:, q * 512:(q + 1) * 512], Pacc[:, q * 512:(q + 1) * 512],
                        ps_P[:, :]),
                           waits=[("pe", pe_pl)], incs=[("ve", 1)])
                    S["psP_free_ve"] = p.v("ve")
                if gen is not None:
                    for _ in gen:
                        pass

            for _ in emit_prep(0):
                pass
            S["ln_done_h1"] = S["ln_done"]
            for b in range(NBLK):
                emit_h1(b)
                emit_h2(b, gen=emit_prep(b + 1) if b + 1 < NBLK else None)
                S["ln_done_h1"] = S["ln_done"]

            for g in range(4):
                p.emit("vector", lambda v, g=g: v.tensor_copy(
                    rts[0:8, :], Pacc[:, g * 512:(g + 1) * 512]),
                       waits=[("pe", S["rts_rd_pe"])], incs=[("ve", 1)])
                ve_stg = p.v("ve")
                for j in range(4):
                    p.emit("tensor", lambda t, j=j: t.matmul(
                        ps_rtt[:, j, :], rts[:, j * 128:(j + 1) * 128], ident[:, 0:8],
                        start=True, stop=True),
                           waits=([("ve", ve_stg), ("ve", S["rtt_evac_ve"])] if j == 0 else []),
                           incs=[("pe", 1)] if j == 3 else [])
                pe_t = p.v("pe")
                S["rts_rd_pe"] = pe_t
                p.emit("vector", lambda v, g=g: v.tensor_copy(
                    PTb[:, g * 4:(g + 1) * 4, :], ps_rtt[:, :, :]),
                       waits=[("pe", pe_t)], incs=[("ve", 1)])
                S["rtt_evac_ve"] = p.v("ve")
            ptb_ve = p.v("ve")
            sc_ps_free = S["psP_free_ve"]
            for q in range(4):
                wp = q % 2
                p.emit("gpsimd", lambda g, q=q, wp=wp: g.dma_start(
                    out=wbuf[:, wp, :, :],
                    in_=w2T[:, q * 512:(q + 1) * 512].rearrange("(k p) m -> p k m", p=128)),
                       waits=[("pe", S["wbuf_rd_pe"][wp])], incs=[("gpd", 16)])
                gp_w = p.v("gpd")
                for ki in range(NKD):
                    w0 = []
                    if ki == 0:
                        w0 = [("gpd", gp_w), ("ve", ptb_ve), ("ve", sc_ps_free)]
                    p.emit("tensor", lambda t, wp=wp, ki=ki: t.matmul(
                        ps_P[:, :], PTb[:, ki, :], wbuf[:, wp, ki, :],
                        start=(ki == 0), stop=(ki == NKD - 1)),
                           waits=w0, incs=[("pe", 1)] if ki == NKD - 1 else [])
                pe_sc = p.v("pe")
                S["wbuf_rd_pe"][wp] = pe_sc
                p.emit("vector", lambda v, q=q: v.tensor_copy(
                    scatacc[:, q * 512:(q + 1) * 512], ps_P[:, :]),
                       waits=[("pe", pe_sc)], incs=[("ve", 1)])
                sc_ps_free = p.v("ve")

            p.mute = False
            if not use_ffn:
                p.emit("vector", lambda v: v.memset(scatacc[:, :], 0.0), incs=[("ve", 1)])
            scat_final_ve = p.v("ve")

            cfin = xin[0:8, 0, :]
            x8l = xin[0:8, 1, :]
            p.emit("sync", lambda s: s.dma_start(out=x8l, in_=x_sl[0:8, :]),
                   waits=[("ve", S["xin_rd_ve"][1]), ("se", S["xin_rd_se"][1])],
                   incs=[("ldx", 16)])
            ld_x8l = p.v("ldx")
            if dbg:
                p.emit("sync", lambda s: s.dma_start(out=dbg1, in_=scatacc[:, :]),
                       waits=[("ve", scat_final_ve)], incs=[("st", 16)])
                p.emit("sync", lambda s: s.dma_start(out=dbg2, in_=s8acc[:, :]),
                       waits=[("ve", ve_s8)], incs=[("st", 16)])
            p.emit("gpsimd", lambda g: g.dma_start(out=ccin[0:8, :], in_=scatacc[:, :]),
                   waits=[("ve", scat_final_ve)], incs=[("gpd", 16)])
            p.emit("gpsimd", lambda g: g.dma_start(out=ccin[8:16, :], in_=s8acc[:, :]),
                   waits=[("ve", ve_s8)], incs=[("gpd", 16)])
            gp_cc = p.v("gpd")
            if use_coll:
                p.emit("gpsimd", lambda g: g.collective_compute(
                    "AllReduce", ALU.add, replica_groups=[list(range(NCORE))],
                    ins=[ccin.ap().opt()], outs=[ccout.ap().opt()]),
                       waits=[("gpd", gp_cc)], incs=[("cc", 1)])
            else:
                p.emit("gpsimd", lambda g: g.dma_start(out=ccout[:, :], in_=ccin[:, :]),
                       waits=[("gpd", gp_cc)], incs=[("cc", 16)])
            p.emit("gpsimd", lambda g: g.dma_start(out=scatacc[:, :], in_=ccout[0:8, :]),
                   waits=[("cc", 1)], incs=[("gpd", 16)])
            p.emit("gpsimd", lambda g: g.dma_start(out=s8acc[:, :], in_=ccout[8:16, :]),
                   incs=[("gpd", 16)])
            gp_cr = p.v("gpd")
            if dbg:
                p.emit("sync", lambda s: s.dma_start(out=dbg3, in_=ccout[:, :]),
                       waits=[("cc", 1), ("gpd", gp_cr)], incs=[("st", 16)])
            p.emit("scalar", lambda s: s.activation(cfin, scatacc[:, :], AF.Silu),
                   waits=[("gpd", gp_cr), ("ve", S["xin_rd_ve"][0]),
                          ("se", S["xin_rd_se"][0])], incs=[("se", 1)])
            se_g8 = p.v("se")
            p.emit("vector", lambda v: v.tensor_mul(cfin, cfin, s8acc[:, :]),
                   waits=[("se", se_g8), ("gpd", gp_cr)])
            p.emit("vector", lambda v: v.tensor_scalar(cfin, cfin, mask8[:, :], None,
                                                       ALU.mult),
                   waits=[("ldx", const_ld)])
            p.emit("vector", lambda v: v.tensor_add(cfin, cfin, x8l),
                   waits=[("ldx", ld_x8l)], incs=[("ve", 1)])
            ve_o8 = p.v("ve")
            p.emit("sync", lambda s: s.dma_start(out=out[0:8, :], in_=cfin),
                   waits=[("ve", ve_o8)], incs=[("st", 16)])
            p.emit("sync", lambda s: s.wait_ge(p.hand["st"], p.v("st")))


            block = root.enter_context(nc.Block())
            p.finalize(block)
    return nc


_NC_CACHE = None


def _get_nc():
    global _NC_CACHE
    if _NC_CACHE is None:
        _NC_CACHE = build_program()
    return _NC_CACHE


def prep_in_maps(L0_attn_hidden_states, L0_attn_W_qkv, L0_attn_W_z, L0_attn_W_a,
                 L0_attn_W_b, L0_attn_conv_w, L0_attn_A_log, L0_attn_dt_bias,
                 L0_attn_norm_w, L0_attn_W_out, L0_ffn_w1, L0_ffn_w3, L0_ffn_w2,
                 L0_ffn_wr):
    f32 = np.float32
    x = np.ascontiguousarray(np.asarray(L0_attn_hidden_states, dtype=f32)).reshape(TOK, D)
    Wqkv = np.asarray(L0_attn_W_qkv, dtype=f32)
    Wz = np.asarray(L0_attn_W_z, dtype=f32)
    Wa = np.asarray(L0_attn_W_a, dtype=f32)
    Wb = np.asarray(L0_attn_W_b, dtype=f32)
    cwf = np.asarray(L0_attn_conv_w, dtype=f32)
    Alog = np.asarray(L0_attn_A_log, dtype=f32)
    dtb = np.asarray(L0_attn_dt_bias, dtype=f32)
    nw = np.asarray(L0_attn_norm_w, dtype=f32)
    Wout = np.asarray(L0_attn_W_out, dtype=f32)
    w1 = np.asarray(L0_ffn_w1, dtype=f32)
    w3 = np.asarray(L0_ffn_w3, dtype=f32)
    w2 = np.asarray(L0_ffn_w2, dtype=f32)
    wr = np.asarray(L0_ffn_wr, dtype=f32)

    w1T = np.ascontiguousarray(w1.T)
    w3T = np.ascontiguousarray(w3.T)
    w2T = np.ascontiguousarray(w2.T)
    wrT = np.ascontiguousarray(wr.T)
    WoutT = np.ascontiguousarray(Wout.T)
    ones = np.ones((128, 128), f32)
    ident = np.eye(128, dtype=f32)
    identb = np.eye(128).astype(ml_dtypes.bfloat16)
    x8 = np.ascontiguousarray(x[0:8, :])

    in_maps = []
    for c in range(NCORE):
        kh0 = 2 * c * 128
        vh0 = 4 * c * 128
        in_maps.append({
            "x_sl": np.ascontiguousarray(x[c * TOKC:(c + 1) * TOKC]),
            "x8": x8,
            "w1T": w1T, "w3T": w3T, "w2T": w2T, "wrT": wrT,
            "wqT": np.ascontiguousarray(Wqkv[kh0:kh0 + 256, :].T),
            "wkT": np.ascontiguousarray(Wqkv[KEY + kh0:KEY + kh0 + 256, :].T),
            "wvT": np.ascontiguousarray(Wqkv[2 * KEY + vh0:2 * KEY + vh0 + 512, :].T),
            "wzT": np.ascontiguousarray(Wz[vh0:vh0 + 512, :].T),
            "wabT": np.ascontiguousarray(
                np.concatenate([Wa[4 * c:4 * c + 4], Wb[4 * c:4 * c + 4]], 0).T),
            "woT": np.ascontiguousarray(WoutT[vh0:vh0 + 512, :]),
            "cw": np.ascontiguousarray(np.concatenate(
                [cwf[kh0:kh0 + 256], cwf[KEY + kh0:KEY + kh0 + 256],
                 cwf[2 * KEY + vh0:2 * KEY + vh0 + 512]], 0)),
            "alog": np.ascontiguousarray(Alog[4 * c:4 * c + 4].reshape(4, 1)),
            "dtb": np.ascontiguousarray(dtb[4 * c:4 * c + 4].reshape(4, 1)),
            "normw": np.ascontiguousarray(nw.reshape(128, 1)),
            "mask8": np.full((8, 1), 1.0 if c == 0 else 0.0, f32),
            "ones": ones, "ident": ident, "identb": identb,
        })
    return in_maps


def kernel(**inputs):
    in_maps = prep_in_maps(**inputs)
    nc = _get_nc()
    res = run_bass_kernel_spmd(nc, in_maps, core_ids=list(range(NCORE)))
    outp = np.concatenate([res.results[c]["out"] for c in range(NCORE)], axis=0)
    return outp.reshape(B, S, D)




# version 15
# speedup vs baseline: 1.0957x; 1.0957x over previous
import numpy as np
import ml_dtypes
from contextlib import ExitStack

import concourse.bass as bass
import concourse.mybir as mybir
from concourse.bass_utils import run_bass_kernel_spmd

F32 = mybir.dt.float32
BF16 = mybir.dt.bfloat16
AF = mybir.ActivationFunctionType
ALU = mybir.AluOpType
AX = mybir.AxisListType

NCORE = 8
B, S, D = 4, 8192, 2048
TOK = B * S
TOKC = TOK // NCORE
NT = TOKC // 128
NBLK = 2
BT = NT // NBLK
NKD = D // 128
HV, HK, DK, DV = 32, 16, 128, 128
KEY, VAL = HK * DK, HV * DV
E8 = 8
RD = 1.0 / D
EPS_LN = 1e-5
EPS_AT = 1e-6
DKI = DK ** (-0.5)

ENGS = ("sync", "gpsimd", "vector", "scalar", "tensor")


class Prog:

    def __init__(self, nc):
        self.nc = nc
        self.ops = {e: [] for e in ENGS}
        self.cnt = {}
        self.hand = {}
        self.mute = False

    def sem(self, ctx, name):
        self.hand[name] = ctx.enter_context(self.nc.semaphore(name))
        self.cnt[name] = 0

    def emit(self, eng, builder, waits=(), incs=()):
        if self.mute:
            return
        rw = [(self.hand[nm], int(v)) for nm, v in waits]
        ri = []
        for nm, n in incs:
            self.cnt[nm] += n
            ri.append((self.hand[nm], n, self.cnt[nm]))
        self.ops[eng].append((builder, rw, ri))

    def v(self, nm):
        return self.cnt[nm]

    def finalize(self, block):
        self_sync = {"vector": "vq", "scalar": "sq"}
        for eng in ENGS:
            ops = self.ops[eng]
            if not ops:
                continue

            def body(e, ops=ops, eng=eng):
                sq = self.hand.get(self_sync.get(eng))
                sq_cnt = 0
                last_edge = None
                for builder, waits, incs in ops:
                    for h, val in waits:
                        if val > 0:
                            e.wait_ge(h, val)
                    if sq is not None and last_edge is not None:
                        e.wait_ge(last_edge[0], last_edge[1])
                    ins = builder(e)
                    for h, n, post in incs:
                        if ins is None:
                            e.sem_inc(h, n)
                        else:
                            ins = ins.then_inc(h, n)
                    if sq is not None:
                        if ins is None:
                            if incs:
                                pass
                        elif incs:
                            last_edge = (incs[0][0], incs[0][2])
                        else:
                            sq_cnt += 1
                            ins.then_inc(sq, 1)
                            last_edge = (sq, sq_cnt)
            getattr(block, eng)(body)


def build_program(use_mini=True, use_ffn=True, use_coll=True, dbg=False):
    mlvl = 7 if use_mini is True else (0 if use_mini is False else int(use_mini))
    nc = bass.Bass("TRN2", target_bir_lowering=False, detect_race_conditions=False)

    x_sl = nc.dram_tensor("x_sl", [TOKC, D], F32, kind="ExternalInput").ap()
    x8 = nc.dram_tensor("x8", [8, D], F32, kind="ExternalInput").ap()
    w1T = nc.dram_tensor("w1T", [D, D], F32, kind="ExternalInput").ap()
    w3T = nc.dram_tensor("w3T", [D, D], F32, kind="ExternalInput").ap()
    w2T = nc.dram_tensor("w2T", [D, D], F32, kind="ExternalInput").ap()
    wrT = nc.dram_tensor("wrT", [D, E8], F32, kind="ExternalInput").ap()
    wqT = nc.dram_tensor("wqT", [D, 256], F32, kind="ExternalInput").ap()
    wkT = nc.dram_tensor("wkT", [D, 256], F32, kind="ExternalInput").ap()
    wvT = nc.dram_tensor("wvT", [D, 512], F32, kind="ExternalInput").ap()
    wzT = nc.dram_tensor("wzT", [D, 512], F32, kind="ExternalInput").ap()
    wabT = nc.dram_tensor("wabT", [D, 8], F32, kind="ExternalInput").ap()
    woT = nc.dram_tensor("woT", [512, D], F32, kind="ExternalInput").ap()
    cwI = nc.dram_tensor("cw", [1024, 4], F32, kind="ExternalInput").ap()
    alogI = nc.dram_tensor("alog", [4, 1], F32, kind="ExternalInput").ap()
    dtbI = nc.dram_tensor("dtb", [4, 1], F32, kind="ExternalInput").ap()
    normwI = nc.dram_tensor("normw", [128, 1], F32, kind="ExternalInput").ap()
    mask8I = nc.dram_tensor("mask8", [8, 1], F32, kind="ExternalInput").ap()
    onesI = nc.dram_tensor("ones", [128, 128], F32, kind="ExternalInput").ap()
    identI = nc.dram_tensor("ident", [128, 128], F32, kind="ExternalInput").ap()
    identbI = nc.dram_tensor("identb", [128, 128], BF16, kind="ExternalInput").ap()
    out = nc.dram_tensor("out", [TOKC, D], F32, kind="ExternalOutput").ap()

    if dbg:
        dbg1 = nc.dram_tensor("dbg1", [8, D], F32, kind="ExternalOutput").ap()
        dbg2 = nc.dram_tensor("dbg2", [8, D], F32, kind="ExternalOutput").ap()
        dbg3 = nc.dram_tensor("dbg3", [16, D], F32, kind="ExternalOutput").ap()
    ccin = nc.dram_tensor("ccin", [16, D], F32)
    ccout = nc.dram_tensor("ccout", [16, D], F32)

    with ExitStack() as root:
        e = root.enter_context
        ones = e(nc.sbuf_tensor("ones_sb", [128, 128], F32))
        ident = e(nc.sbuf_tensor("ident_sb", [128, 128], F32))
        identb = e(nc.sbuf_tensor("identb_sb", [128, 128], BF16))
        mask8 = e(nc.sbuf_tensor("mask8_sb", [8, 1], F32))

        cst = e(nc.sbuf_tensor("cst", [128, 4], F32))

        p = Prog(nc)
        for s_ in ("ldx", "gpd", "pe", "ve", "se", "st", "cc", "vq", "sq"):
            p.sem(root, s_)

        p.emit("vector", lambda v: v.memset(cst[:, 0:1], EPS_LN))
        p.emit("vector", lambda v: v.memset(cst[:, 1:2], EPS_AT))
        p.emit("vector", lambda v: v.memset(cst[:, 2:3], 1.0), incs=[("ve", 1)])

        p.emit("sync", lambda s: s.dma_start(out=ones[:, :], in_=onesI), incs=[("ldx", 16)])
        p.emit("sync", lambda s: s.dma_start(out=ident[:, :], in_=identI), incs=[("ldx", 16)])
        p.emit("sync", lambda s: s.dma_start(out=identb[:, :], in_=identbI), incs=[("ldx", 16)])
        p.emit("sync", lambda s: s.dma_start(out=mask8[:, :], in_=mask8I), incs=[("ldx", 16)])
        const_ld = p.v("ldx")

        p.mute = mlvl < 1
        with ExitStack() as ma:
            em = ma.enter_context
            wq = em(nc.sbuf_tensor("wq", [128, NKD, 256], F32))
            wk = em(nc.sbuf_tensor("wk", [128, NKD, 256], F32))
            wv = em(nc.sbuf_tensor("wv", [128, NKD, 512], F32))
            wz = em(nc.sbuf_tensor("wz", [128, NKD, 512], F32))
            wab = em(nc.sbuf_tensor("wab", [128, NKD, 8], F32))
            wo = em(nc.sbuf_tensor("wo", [128, 4, D], F32))
            cw = em(nc.sbuf_tensor("cw_sb", [128, 8, 4], F32))
            alog = em(nc.sbuf_tensor("alog_sb", [4, 1], F32))
            dtb = em(nc.sbuf_tensor("dtb_sb", [4, 1], F32))
            normw = em(nc.sbuf_tensor("normw_sb", [128, 1], F32))
            x8s = em(nc.sbuf_tensor("x8s", [8, D], F32))
            xn8 = em(nc.sbuf_tensor("xn8", [128, D], F32))
            st8 = em(nc.sbuf_tensor("st8", [8, 8], F32))
            xn8fm = em(nc.sbuf_tensor("xn8fm", [128, NKD, 8], F32))
            rawp = em(nc.sbuf_tensor("rawp", [128, 8, 11], F32))
            cvt = em(nc.sbuf_tensor("cvt", [128, 8, 8], F32))
            mx = em(nc.sbuf_tensor("mx", [128, 8, 8], F32))
            zt = em(nc.sbuf_tensor("zt", [128, 4, 8], F32))
            absb = em(nc.sbuf_tensor("absb", [4, 16], F32))
            sqs = em(nc.sbuf_tensor("sqs", [128, 8], F32))
            nr = em(nc.sbuf_tensor("nr", [128, 32], F32))
            qn = em(nc.sbuf_tensor("qn", [128, 4, 8], F32))
            a2 = em(nc.sbuf_tensor("a2", [4, 8], F32))
            spx = em(nc.sbuf_tensor("spx", [4, 8], F32))
            spl = em(nc.sbuf_tensor("spl", [4, 8], F32))
            spm = em(nc.sbuf_tensor("spm", [4, 8], F32))
            gd = em(nc.sbuf_tensor("gd", [4, 8], F32))
            Es = em(nc.sbuf_tensor("Es", [4, 8], F32))
            Bs = em(nc.sbuf_tensor("Bs", [128, 8], F32))
            eA = em(nc.sbuf_tensor("eA", [4, 1], F32))
            bT = em(nc.sbuf_tensor("bT", [8, 4], F32))
            erow = em(nc.sbuf_tensor("erow", [128, 32], F32))
            Ebc = em(nc.sbuf_tensor("Ebc", [128, 4, 8], F32))
            ktr = em(nc.sbuf_tensor("ktr", [8, 128], F32))
            vtr = em(nc.sbuf_tensor("vtr", [8, 128], F32))
            kbt = em(nc.sbuf_tensor("kbt", [8, 128], F32))
            krow = em(nc.sbuf_tensor("krow", [128, 1024], F32))
            vrow = em(nc.sbuf_tensor("vrow", [1, 1024], F32))
            dls = em(nc.sbuf_tensor("dls", [128, 128], F32))
            Sst = em(nc.sbuf_tensor("Sst", [128, 128], F32))
            Oall = em(nc.sbuf_tensor("Oall", [128, 4, 8], F32))
            Ofin = em(nc.sbuf_tensor("Ofin", [128, 4, 8], F32))
            szal = em(nc.sbuf_tensor("szal", [128, 4, 8], F32))
            s8stg = em(nc.sbuf_tensor("s8stg", [8, 512], F32))

            def gld(dst, src):
                p.emit("gpsimd", lambda g, d=dst, s=src: g.dma_start(out=d, in_=s),
                       incs=[("gpd", 16)])
            gld(x8s[:, :], x8)
            gld(cw[:, :, :], cwI.rearrange("(c p) k -> p c k", p=128))
            gld(alog[:, :], alogI)
            gld(dtb[:, :], dtbI)
            gld(normw[:, :], normwI)
            mini_g1 = p.v("gpd")
            gld(wq[:, :, :], wqT.rearrange("(k p) m -> p k m", p=128))
            gld(wk[:, :, :], wkT.rearrange("(k p) m -> p k m", p=128))
            gld(wab[:, :, :], wabT.rearrange("(k p) m -> p k m", p=128))
            mini_g2 = p.v("gpd")
            gld(wv[:, :, :], wvT.rearrange("(k p) m -> p k m", p=128))
            gld(wz[:, :, :], wzT.rearrange("(k p) m -> p k m", p=128))
            mini_g3 = p.v("gpd")
            mini_ld = p.v("gpd")
            p.emit("sync", lambda s: s.dma_start(out=wo[:, :, :],
                                                 in_=woT.rearrange("(h p) m -> p h m", p=128)),
                   incs=[("ldx", 16)])
            ldx_wo = p.v("ldx")

            p.emit("vector", lambda v: v.memset(xn8[:, :], 0.0))
            p.emit("vector", lambda v: v.memset(Bs[:, :], 0.0))
            p.emit("vector", lambda v: v.memset(nr[:, :], 0.0))
            p.emit("vector", lambda v: v.memset(erow[:, :], 0.0))
            p.emit("vector", lambda v: v.memset(krow[:, :], 0.0))
            p.emit("vector", lambda v: v.memset(dls[:, :], 0.0), incs=[("ve", 1)])

            with ExitStack() as mp1:
                e1 = mp1.enter_context
                ps_t8 = e1(nc.psum_tensor("ps_t8", [128, NKD, 8], F32))
                ps_p = e1(nc.psum_tensor("ps_p", [128, 8], F32))
                ps_p2 = e1(nc.psum_tensor("ps_p2", [128, 8], F32))
                ps_ab = e1(nc.psum_tensor("ps_ab", [4, 16], F32))
                ps_n = e1(nc.psum_tensor("ps_n", [1, 64], F32))
                ps_b = e1(nc.psum_tensor("ps_b", [128, 64], F32))
                ps_t2 = e1(nc.psum_tensor("ps_t2", [8, 128], F32))

                p.emit("vector", lambda v: v.reduce_sum(st8[:, 0:1], x8s[:, :], axis=AX.X, negate=True),
                       waits=[("gpd", mini_g1)])
                p.emit("vector", lambda v: v.tensor_scalar_mul(st8[:, 1:2], st8[:, 0:1], RD))
                ve_nm = p.v("ve")
                p.emit("vector", lambda v: None, incs=[("ve", 1)])
                p.emit("scalar", lambda s: s.activation(xn8[0:8, :], x8s[:, :], AF.Square,
                                                        bias=st8[:, 1:2], scale=1.0,
                                                        accum_out=st8[:, 2:3]),
                       waits=[("ve", p.v("ve"))])
                p.emit("scalar", lambda s: s.activation(st8[:, 3:4], st8[:, 2:3], AF.Sqrt,
                                                        bias=cst[0:8, 0:1], scale=RD),
                       incs=[("se", 1)])
                se_sv = p.v("se")
                p.emit("vector", lambda v: v.reciprocal(st8[:, 4:5], st8[:, 3:4]),
                       waits=[("se", se_sv)])
                p.emit("vector", lambda v: v.tensor_scalar(xn8[0:8, :], x8s[:, :], st8[:, 1:2],
                                                           st8[:, 4:5], ALU.add, ALU.mult),
                       incs=[("ve", 1)])
                ve_xn8 = p.v("ve")

                p.mute = mlvl < 2
                for j in range(NKD):
                    p.emit("tensor", lambda t, j=j: t.matmul(ps_t8[:, j, :], xn8[:, j * 128:(j + 1) * 128],
                                                             ident[:, 0:8], start=True, stop=True),
                           waits=[("ve", ve_xn8), ("ldx", const_ld)] if j == 0 else [],
                           incs=[("pe", 1)] if j == NKD - 1 else [])
                pe_t8 = p.v("pe")
                p.emit("vector", lambda v: v.tensor_copy(xn8fm[:, :, :], ps_t8[:, :, :]),
                       waits=[("pe", pe_t8)], incs=[("ve", 1)])
                ve_fm = p.v("ve")

                p.emit("vector", lambda v: v.memset(rawp[:, :, :], 0.0), incs=[("ve", 1)])
                ve_pad = p.v("ve")

                proj = []
                for mo in range(2):
                    proj.append((wq, mo, ("rawp", mo)))
                for mo in range(2):
                    proj.append((wk, mo, ("rawp", 2 + mo)))
                for mo in range(4):
                    proj.append((wv, mo, ("rawp", 4 + mo)))
                for mo in range(4):
                    proj.append((wz, mo, ("zt", mo)))
                pe_evac = {}
                for i, (wt, mo, dst) in enumerate(proj):
                    par = i % 2
                    ps = ps_p if par == 0 else ps_p2
                    w8 = []
                    if i == 0:
                        w8.append(("ve", ve_fm))
                        w8.append(("gpd", mini_g2))
                    if i == 4:
                        w8.append(("gpd", mini_g3))
                    if par in pe_evac:
                        w8.append(("ve", pe_evac[par]))
                    for ki in range(NKD):
                        p.emit("tensor",
                               lambda t, wt=wt, mo=mo, ki=ki, ps=ps: t.matmul(
                                   ps[:, :], wt[:, ki, mo * 128:(mo + 1) * 128], xn8fm[:, ki, :],
                                   start=(ki == 0), stop=(ki == NKD - 1)),
                               waits=w8 if ki == 0 else [],
                               incs=[("pe", 1)] if ki == NKD - 1 else [])
                    pe_mm = p.v("pe")
                    kind, ch = dst
                    if kind == "rawp":
                        p.emit("vector", lambda v, ch=ch, ps=ps: v.tensor_copy(rawp[:, ch, 3:11], ps[:, :]),
                               waits=[("pe", pe_mm), ("ve", ve_pad)], incs=[("ve", 1)])
                    else:
                        p.emit("vector", lambda v, ch=ch, ps=ps: v.tensor_copy(zt[:, ch, :], ps[:, :]),
                               waits=[("pe", pe_mm)], incs=[("ve", 1)])
                    pe_evac[par] = p.v("ve")
                for half in range(2):
                    for ki in range(NKD):
                        p.emit("tensor",
                               lambda t, ki=ki, half=half: t.matmul(
                                   ps_ab[:, 8 * half:8 * half + 8],
                                   wab[:, ki, 4 * half:4 * half + 4], xn8fm[:, ki, :],
                                   start=(ki == 0), stop=(ki == NKD - 1)),
                               waits=[("gpd", mini_g2)] if (half == 0 and ki == 0) else [],
                               incs=[("pe", 1)] if ki == NKD - 1 else [])
                pe_ab = p.v("pe")
                p.emit("vector", lambda v: v.tensor_copy(absb[:, :], ps_ab[:, :]),
                       waits=[("pe", pe_ab)], incs=[("ve", 1)])
                ve_ab = p.v("ve")

                p.mute = mlvl < 3
                for ch in range(8):
                    p.emit("vector", lambda v, ch=ch: v.tensor_scalar(
                        cvt[:, ch, :], rawp[:, ch, 0:8], cw[:, ch, 0:1], None, ALU.mult))
                    for k in range(1, 4):
                        p.emit("vector", lambda v, ch=ch, k=k: v.scalar_tensor_tensor(
                            cvt[:, ch, :], rawp[:, ch, k:k + 8], cw[:, ch, k:k + 1],
                            cvt[:, ch, :], ALU.mult, ALU.add))
                p.emit("vector", lambda v: None, incs=[("ve", 1)])
                ve_cvt = p.v("ve")
                p.emit("scalar", lambda s: s.activation(mx[:, :, :], cvt[:, :, :], AF.Silu),
                       waits=[("ve", ve_cvt)], incs=[("se", 1)])
                se_mx = p.v("se")

                for ch in range(4):
                    p.emit("scalar", lambda s, ch=ch: s.activation(sqs[:, :], mx[:, ch, :], AF.Square),
                           waits=[("se", se_mx)], incs=[("se", 1)])
                    se_sq = p.v("se")
                    p.emit("tensor", lambda t, ch=ch: t.matmul(ps_n[0:1, 8 * ch:8 * ch + 8],
                                                               ones[:, 0:1], sqs[:, :],
                                                               start=True, stop=True),
                           waits=[("se", se_sq)], incs=[("pe", 1)])
                    pe_n = p.v("pe")
                    p.emit("scalar", lambda s, ch=ch: s.activation(nr[0:1, 8 * ch:8 * ch + 8],
                                                                   ps_n[0:1, 8 * ch:8 * ch + 8], AF.Sqrt),
                           waits=[("pe", pe_n)], incs=[("se", 1)])
                    se_nr = p.v("se")
                    p.emit("vector", lambda v, ch=ch: v.tensor_scalar_max(
                        nr[0:1, 8 * ch:8 * ch + 8], nr[0:1, 8 * ch:8 * ch + 8], 1e-6),
                           waits=[("se", se_nr)])
                    p.emit("vector", lambda v, ch=ch: v.reciprocal(
                        nr[0:1, 8 * ch:8 * ch + 8], nr[0:1, 8 * ch:8 * ch + 8]))
                    if ch < 2:
                        p.emit("vector", lambda v, ch=ch: v.tensor_scalar_mul(
                            nr[0:1, 8 * ch:8 * ch + 8], nr[0:1, 8 * ch:8 * ch + 8], DKI))
                    p.emit("vector", lambda v: None, incs=[("ve", 1)])
                    ve_r = p.v("ve")
                    p.emit("tensor", lambda t, ch=ch: t.matmul(ps_b[:, 8 * ch:8 * ch + 8],
                                                               ones[:, :], nr[:, 8 * ch:8 * ch + 8],
                                                               start=True, stop=True),
                           waits=[("ve", ve_r)], incs=[("pe", 1)])
                    pe_b = p.v("pe")
                    p.emit("vector", lambda v, ch=ch: v.tensor_mul(qn[:, ch, :], mx[:, ch, :],
                                                                   ps_b[:, 8 * ch:8 * ch + 8]),
                           waits=[("pe", pe_b)], incs=[("ve", 1)])
                ve_qn = p.v("ve")

                p.mute = mlvl < 4
                p.emit("vector", lambda v: v.tensor_scalar_add(a2[:, :], absb[0:4, 0:8], dtb[:, :]),
                       waits=[("ve", ve_ab)], incs=[("ve", 1)])
                ve_a2 = p.v("ve")
                p.emit("scalar", lambda s: s.activation(spx[:, :], a2[:, :], AF.Abs),
                       waits=[("ve", ve_a2)])
                p.emit("scalar", lambda s: s.activation(spx[:, :], spx[:, :], AF.Exp, scale=-1.0))
                p.emit("scalar", lambda s: s.activation(spl[:, :], spx[:, :], AF.Ln, bias=cst[0:4, 2:3]),
                       incs=[("se", 1)])
                se_spl = p.v("se")
                p.emit("vector", lambda v: v.tensor_scalar_max(spm[:, :], a2[:, :], 0.0))
                p.emit("vector", lambda v: v.tensor_add(spm[:, :], spm[:, :], spl[:, :]),
                       waits=[("se", se_spl)], incs=[("ve", 1)])
                ve_sp = p.v("se")
                p.emit("scalar", lambda s: s.activation(eA[:, :], alog[:, :], AF.Exp),
                       incs=[("se", 1)])
                se_eA = p.v("se")
                p.emit("vector", lambda v: v.tensor_scalar_mul(eA[:, :], eA[:, :], -1.0),
                       waits=[("se", se_eA)])
                p.emit("vector", lambda v: v.tensor_scalar(gd[:, :], spm[:, :], eA[:, :], None, ALU.mult),
                       incs=[("ve", 1)])
                ve_gd = p.v("ve")
                p.emit("scalar", lambda s: s.activation(Es[:, :], gd[:, :], AF.Exp),
                       waits=[("ve", ve_gd)])
                p.emit("scalar", lambda s: s.activation(Bs[0:4, :], absb[0:4, 8:16], AF.Sigmoid),
                       incs=[("se", 1)])
                se_EB = p.v("se")

                p.emit("tensor", lambda t: t.matmul(ps_t2[0:8, 0:4], Bs[:, :], ident[:, 0:4],
                                                    start=True, stop=True),
                       waits=[("se", se_EB)], incs=[("pe", 1)])
                pe_bT = p.v("pe")
                p.emit("vector", lambda v: v.tensor_copy(bT[:, :], ps_t2[0:8, 0:4]),
                       waits=[("pe", pe_bT)], incs=[("ve", 1)])
                ve_bT = p.v("ve")
                p.emit("gpsimd", lambda g: g.dma_start(out=erow[0:1, :], in_=Es[:, :]),
                       waits=[("se", se_EB)], incs=[("gpd", 16)])
                gp_er = p.v("gpd")

                ve_ebc = 0
                for h in range(4):
                    p.emit("tensor", lambda t, h=h: t.matmul(ps_b[:, 8 * h:8 * h + 8], ones[:, :],
                                                             erow[:, 8 * h:8 * h + 8],
                                                             start=True, stop=True),
                           waits=([("gpd", gp_er), ("ve", ve_qn)] if h == 0
                                  else [("ve", ve_ebc)]),
                           incs=[("pe", 1)])
                    pe_eb = p.v("pe")
                    p.emit("vector", lambda v, h=h: v.tensor_copy(Ebc[:, h, :], ps_b[:, 8 * h:8 * h + 8]),
                           waits=[("pe", pe_eb)], incs=[("ve", 1)])
                    ve_ebc = p.v("ve")

            with ExitStack() as mp2:
                e2 = mp2.enter_context
                ps_t2b = e2(nc.psum_tensor("ps_t2b", [8, 128], F32))
                ps_kv = e2(nc.psum_tensor("ps_kv", [1, 128], F32))
                ps_ou = e2(nc.psum_tensor("ps_ou", [128, 128], F32))
                ps_oc = e2(nc.psum_tensor("ps_oc", [128, 1], F32))
                ps_n2 = e2(nc.psum_tensor("ps_n2", [1, 32], F32))
                ps_b2 = e2(nc.psum_tensor("ps_b2", [128, 32], F32))
                ps_s8 = e2(nc.psum_tensor("ps_s8", [8, 512], F32))

                p.mute = mlvl < 5
                pe_head_end = 0
                ve_dl = 0
                ve_vt = ve_bT
                for h in range(4):
                    kh = h // 2
                    p.emit("tensor", lambda t, kh=kh: t.matmul(ps_t2b[:, :], qn[:, 2 + kh, :],
                                                               ident[:, :], start=True, stop=True),
                           waits=[("ve", ve_bT), ("ve", ve_vt)], incs=[("pe", 1)])
                    pe_kt = p.v("pe")
                    p.emit("vector", lambda v: v.tensor_copy(ktr[:, :], ps_t2b[:, :]),
                           waits=[("pe", pe_kt), ("gpd", p.v("gpd"))], incs=[("ve", 1)])
                    ve_kt = p.v("ve")
                    p.emit("vector", lambda v, h=h: v.tensor_scalar(kbt[:, :], ktr[:, :],
                                                                    bT[:, h:h + 1], None, ALU.mult),
                           incs=[("ve", 1)])
                    ve_kb = p.v("ve")
                    p.emit("gpsimd", lambda g: g.dma_start(out=krow[0:1, :], in_=kbt[:, :]),
                           waits=[("ve", ve_kb), ("pe", pe_head_end)], incs=[("gpd", 16)])
                    p.emit("tensor", lambda t, h=h: t.matmul(ps_t2b[:, :], mx[:, 4 + h, :],
                                                             ident[:, :], start=True, stop=True),
                           waits=[("ve", ve_kt)], incs=[("pe", 1)])
                    pe_vt = p.v("pe")
                    p.emit("vector", lambda v: v.tensor_copy(vtr[:, :], ps_t2b[:, :]),
                           waits=[("pe", pe_vt)], incs=[("ve", 1)])
                    ve_vt = p.v("ve")
                    p.emit("gpsimd", lambda g: g.dma_start(out=vrow[:, :], in_=vtr[:, :]),
                           waits=[("ve", ve_vt), ("ve", ve_dl)], incs=[("gpd", 16)])
                    gp_head = p.v("gpd")
                    p.emit("vector", lambda v: v.memset(Sst[:, :], 0.0),
                           waits=[("gpd", gp_head), ("pe", pe_head_end)], incs=[("ve", 1)])
                    for t_ in range(8):
                        ve_S = p.v("ve")
                        p.emit("vector", lambda v, h=h, t_=t_: v.tensor_scalar_mul(
                            Sst[:, :], Sst[:, :], Ebc[:, h, t_:t_ + 1]), incs=[("ve", 1)])
                        ve_dec = p.v("ve")
                        p.emit("tensor", lambda t, kh=kh, t_=t_: t.matmul(
                            ps_kv[:, :], qn[:, 2 + kh, t_:t_ + 1], Sst[:, :], start=True, stop=True),
                               waits=[("ve", ve_dec)], incs=[("pe", 1)])
                        pe_kv = p.v("pe")
                        p.emit("vector", lambda v, t_=t_: v.scalar_tensor_tensor(
                            dls[0:1, :], ps_kv[:, :], -1.0, vrow[0:1, 128 * t_:128 * (t_ + 1)],
                            ALU.mult, ALU.add),
                               waits=[("pe", pe_kv)], incs=[("ve", 1)])
                        ve_dl = p.v("ve")
                        p.emit("tensor", lambda t, t_=t_: t.matmul(
                            ps_ou[:, :], krow[:, 128 * t_:128 * (t_ + 1)],
                            dls[:, :], start=True, stop=True),
                               waits=[("ve", ve_dl)], incs=[("pe", 1)])
                        pe_ou = p.v("pe")
                        p.emit("vector", lambda v: v.tensor_add(Sst[:, :], Sst[:, :], ps_ou[:, :]),
                               waits=[("pe", pe_ou)], incs=[("ve", 1)])
                        ve_Su = p.v("ve")
                        p.emit("tensor", lambda t, kh=kh, t_=t_: t.matmul(
                            ps_oc[:, :], Sst[:, :], qn[:, kh, t_:t_ + 1], start=True, stop=True),
                               waits=[("ve", ve_Su)], incs=[("pe", 1)])
                        pe_oc = p.v("pe")
                        p.emit("vector", lambda v, h=h, t_=t_: v.tensor_copy(
                            Oall[:, h, t_:t_ + 1], ps_oc[:, :]),
                               waits=[("pe", pe_oc)], incs=[("ve", 1)])
                    pe_head_end = p.v("pe")
                ve_rec = p.v("ve")

                p.mute = mlvl < 6
                p.emit("scalar", lambda s: s.activation(szal[:, :, :], zt[:, :, :], AF.Silu),
                       waits=[("ve", pe_evac.get(0, 0)), ("ve", pe_evac.get(1, 0))], incs=[("se", 1)])
                for h in range(4):
                    p.emit("scalar", lambda s, h=h: s.activation(sqs[:, :], Oall[:, h, :], AF.Square),
                           waits=[("ve", ve_rec)], incs=[("se", 1)])
                    se_sq2 = p.v("se")
                    p.emit("tensor", lambda t, h=h: t.matmul(ps_n2[0:1, 8 * h:8 * h + 8], ones[:, 0:1],
                                                             sqs[:, :], start=True, stop=True),
                           waits=[("se", se_sq2)], incs=[("pe", 1)])
                    pe_n2 = p.v("pe")
                    p.emit("scalar", lambda s, h=h: s.activation(nr[0:1, 8 * h:8 * h + 8],
                                                                 ps_n2[0:1, 8 * h:8 * h + 8], AF.Sqrt,
                                                                 scale=1.0 / DV, bias=cst[0:1, 1:2]),
                           waits=[("pe", pe_n2)], incs=[("se", 1)])
                    se_nr2 = p.v("se")
                    p.emit("vector", lambda v, h=h: v.reciprocal(nr[0:1, 8 * h:8 * h + 8],
                                                                 nr[0:1, 8 * h:8 * h + 8]),
                           waits=[("se", se_nr2)], incs=[("ve", 1)])
                    ve_r2 = p.v("ve")
                    p.emit("tensor", lambda t, h=h: t.matmul(ps_b2[:, 8 * h:8 * h + 8], ones[:, :],
                                                             nr[:, 8 * h:8 * h + 8], start=True, stop=True),
                           waits=[("ve", ve_r2)], incs=[("pe", 1)])
                    pe_b2 = p.v("pe")
                    p.emit("vector", lambda v, h=h: v.scalar_tensor_tensor(
                        Ofin[:, h, :], Oall[:, h, :], normw[:, :], ps_b2[:, 8 * h:8 * h + 8],
                        ALU.mult, ALU.mult),
                           waits=[("pe", pe_b2)])
                    p.emit("vector", lambda v, h=h: v.tensor_mul(Ofin[:, h, :], Ofin[:, h, :],
                                                                 szal[:, h, :]),
                           waits=[("se", p.v("se"))], incs=[("ve", 1)])
                ve_of = p.v("ve")

                p.mute = mlvl < 7
                ve_s8ev = 0
                gp_s8 = p.v("gpd")
                for fo in range(4):
                    for h in range(4):
                        w8_ = []
                        if h == 0:
                            w8_.append(("ve", ve_of if fo == 0 else ve_s8ev))
                            if fo == 0:
                                w8_.append(("ldx", ldx_wo))
                        p.emit("tensor", lambda t, fo=fo, h=h: t.matmul(
                            ps_s8[:, :], Ofin[:, h, :], wo[:, h, fo * 512:(fo + 1) * 512],
                            start=(h == 0), stop=(h == 3)),
                               waits=w8_,
                               incs=[("pe", 1)] if h == 3 else [])
                    pe_s8 = p.v("pe")
                    p.emit("vector", lambda v, fo=fo: v.tensor_copy(s8stg[:, :], ps_s8[:, :]),
                           waits=[("pe", pe_s8), ("gpd", gp_s8)], incs=[("ve", 1)])
                    ve_s8ev = p.v("ve")
                    p.emit("gpsimd", lambda g, fo=fo: g.dma_start(
                        out=ccin[8:16, fo * 512:(fo + 1) * 512], in_=s8stg[:, :]),
                           waits=[("ve", ve_s8ev)], incs=[("gpd", 16)])
                    gp_s8 = p.v("gpd")
                ve_s8 = p.v("ve")
                if mlvl < 7:
                    p.mute = False
                    p.emit("vector", lambda v: v.memset(s8stg[:, :], 0.0), incs=[("ve", 1)])
                    ve_z8 = p.v("ve")
                    for fo in range(4):
                        p.emit("gpsimd", lambda g, fo=fo: g.dma_start(
                            out=ccin[8:16, fo * 512:(fo + 1) * 512], in_=s8stg[:, :]),
                               waits=[("ve", ve_z8)], incs=[("gpd", 16)])

        p.mute = False
        mini_end = {"ve": p.v("ve"), "se": p.v("se"), "pe": p.v("pe"),
                    "gpd": p.v("gpd")}

        with ExitStack() as ff:
            ef = ff.enter_context
            xin = ef(nc.sbuf_tensor("xin", [128, 3, D], F32))
            xntm = ef(nc.sbuf_tensor("xntm", [128, 2, D], BF16))
            bufX = ef(nc.sbuf_tensor("bufX", [128, NKD, 2048], BF16))
            bufH = ef(nc.sbuf_tensor("bufH", [128, NKD, 2048], BF16))
            wbuf = ef(nc.sbuf_tensor("wbuf", [128, 2, NKD, 512], BF16))
            ybuf = ef(nc.sbuf_tensor("ybuf", [128, 3, 512], BF16))
            rts = ef(nc.sbuf_tensor("rts", [128, 512], F32))
            rtf32 = ef(nc.sbuf_tensor("rtf32", [128, BT, 8], BF16))
            wgtb = ef(nc.sbuf_tensor("wgtb", [128, 2, BT, 8], BF16))
            wrs = ef(nc.sbuf_tensor("wrs", [128, NKD, 8], BF16))
            Pacc = ef(nc.sbuf_tensor("Pacc", [8, D], F32))
            scatacc = Pacc
            PTb = ef(nc.sbuf_tensor("PTb", [128, NKD, 8], BF16))
            nmean = ef(nc.sbuf_tensor("nmean", [128, 3], F32))
            vsum = ef(nc.sbuf_tensor("vsum", [128, 3], F32))
            svt = ef(nc.sbuf_tensor("svt", [128, 3], F32))
            rstd = ef(nc.sbuf_tensor("rstd", [128, 3], F32))
            Et = ef(nc.sbuf_tensor("Et", [128, 8], F32))
            E2t = ef(nc.sbuf_tensor("E2t", [128, 8], F32))
            msk1 = ef(nc.sbuf_tensor("msk1", [128, 8], F32))
            msk2 = ef(nc.sbuf_tensor("msk2", [128, 8], F32))
            rsc = ef(nc.sbuf_tensor("rsc", [128, 8], F32))

            ps_h = ef(nc.psum_tensor("ps_h", [128, 2, 512], F32))
            ps_trA = ef(nc.psum_tensor("ps_trA", [128, 512], BF16))
            ps_trB = ef(nc.psum_tensor("ps_trB", [128, 512], BF16))
            ps_rt = ef(nc.psum_tensor("ps_rt", [8, 512], F32))
            ps_rtt = ef(nc.psum_tensor("ps_rtt", [128, 4, 8], F32))
            ps_P = ef(nc.psum_tensor("ps_P", [8, 512], F32))

            p.mute = not use_ffn
            p.emit("gpsimd", lambda g: g.dma_start(out=wrs[:, :, :],
                                                   in_=wrT.rearrange("(k p) m -> p k m", p=128)),
                   waits=[("ve", mini_end["ve"]), ("se", mini_end["se"]),
                          ("pe", mini_end["pe"])],
                   incs=[("gpd", 16)])
            gp_wr = p.v("gpd")
            p.emit("vector", lambda v: v.memset(rts[:, :], 0.0),
                   waits=[("se", mini_end["se"]), ("pe", mini_end["pe"])])
            p.emit("vector", lambda v: v.memset(Pacc[:, :], 0.0), incs=[("ve", 1)])
            zpad_ve = p.v("ve")

            S = {
                "xin_store": [0, 0, 0],
                "xin_rd_ve": [0, 0, 0], "xin_rd_se": [0, 0, 0],
                "xntm_rd_pe": [0, 0], "stat_rd_se": [0, 0, 0],
                "ps_h_rd": [0, 0], "ps_h_rd_sem": ["se", "se"],
                "wbuf_rd_pe": [0, 0],
                "bufH_rd_pe": 0, "bufH_wr_se": 0,
                "ps_trX_rd": [("ve", 0), ("ve", 0)],
                "rt_ps_free_ve": zpad_ve, "rts_rd_pe": 0, "rtt_evac_ve": 0,
                "psP_free_ve": zpad_ve,
                "yb_se": [0, 0, 0], "ybuf_rd_pe": [0, 0, 0],
                "wgtb_wr_ve": 0, "ln_done": [0, 0, 0],
            }

            ld_marks = {}

            def emit_prep(b):
                base = b * BT

                def emit_load(u):
                    par = u % 3
                    gt = base + u
                    rows = (gt * 128, (gt + 1) * 128)
                    lw = [("st", S["xin_store"][par]), ("ve", S["xin_rd_ve"][par]),
                          ("se", S["xin_rd_se"][par])]
                    if b == 0 and u < 3:
                        lw += [("ve", mini_end["ve"]), ("se", mini_end["se"]),
                               ("pe", mini_end["pe"]), ("gpd", mini_end["gpd"])]
                    p.emit("sync", lambda s, par=par, rows=rows: s.dma_start(
                        out=xin[:, par, :], in_=x_sl[rows[0]:rows[1], :]),
                           waits=lw, incs=[("ldx", 16)])
                    ld_marks[(b, u)] = p.v("ldx")

                for tt in range(BT):
                    gt = base + tt
                    par = tt % 3
                    par2 = tt % 2
                    if tt == 0:
                        for u in range(3):
                            emit_load(u)
                    ld_t = ld_marks[(b, tt)]
                    if gt == 0:
                        p.emit("sync", lambda s, par=par: s.dma_start(
                            out=out[8:128, :], in_=xin[8:128, par, :]),
                               waits=[("ldx", ld_t)], incs=[("st", 16)])
                    else:
                        p.emit("sync", lambda s, par=par, gt=gt: s.dma_start(
                            out=out[gt * 128:(gt + 1) * 128, :], in_=xin[:, par, :]),
                               waits=[("ldx", ld_t)], incs=[("st", 16)])
                    S["xin_store"][par] = p.v("st")
                    p.emit("vector", lambda v, par=par: v.reduce_sum(
                        nmean[:, par:par + 1], xin[:, par, :], axis=AX.X, negate=True),
                           waits=[("ldx", ld_t), ("se", S["stat_rd_se"][par])])
                    p.emit("vector", lambda v, par=par: v.tensor_scalar_mul(
                        nmean[:, par:par + 1], nmean[:, par:par + 1], RD), incs=[("ve", 1)])
                    ve_nm = p.v("ve")
                    p.emit("scalar", lambda s, par=par, par2=par2: s.activation(
                        xntm[:, par2, :], xin[:, par, :], AF.Square, bias=nmean[:, par:par + 1],
                        accum_out=vsum[:, par:par + 1]),
                           waits=[("ve", ve_nm), ("pe", S["xntm_rd_pe"][par2])])
                    p.emit("scalar", lambda s, par=par: s.activation(
                        svt[:, par:par + 1], vsum[:, par:par + 1], AF.Sqrt, scale=RD, bias=cst[:, 0:1]),
                           incs=[("se", 1)])
                    se_sv = p.v("se")
                    S["xin_rd_se"][par] = se_sv
                    S["stat_rd_se"][par] = se_sv
                    p.emit("vector", lambda v, par=par: v.reciprocal(
                        rstd[:, par:par + 1], svt[:, par:par + 1]),
                           waits=[("se", se_sv)])
                    p.emit("vector", lambda v, par=par, par2=par2: v.tensor_scalar(
                        xntm[:, par2, :], xin[:, par, :], nmean[:, par:par + 1],
                        rstd[:, par:par + 1], ALU.add, ALU.mult),
                           incs=[("ve", 1)])
                    ve_norm = p.v("ve")
                    S["xin_rd_ve"][par] = ve_norm
                    for gidx in range(4):
                        bank = ps_trA if gidx % 2 == 0 else ps_trB
                        bwait = S["ps_trX_rd"][gidx % 2]
                        for j4 in range(4):
                            j = gidx * 4 + j4
                            p.emit("tensor", lambda t, par2=par2, j=j, j4=j4, bank=bank: t.transpose(
                                bank[:, j4 * 128:(j4 + 1) * 128], xntm[:, par2, j * 128:(j + 1) * 128],
                                identb[:, :]),
                                   waits=([("ve", ve_norm), (bwait[0], bwait[1])] if j4 == 0 else []),
                                   incs=[("pe", 1)] if j4 == 3 else [])
                        pe_tr = p.v("pe")
                        eng = "vector" if gidx % 2 == 0 else "scalar"
                        sem = "ve" if gidx % 2 == 0 else "se"
                        if eng == "vector":
                            p.emit(eng, lambda ee, tt=tt, gidx=gidx, bank=bank: ee.tensor_copy(
                                bufX[:, gidx * 4:(gidx + 1) * 4, tt * 128:(tt + 1) * 128],
                                bank[:, :].rearrange("p (j t) -> p j t", j=4)),
                                   waits=[("pe", pe_tr)], incs=[(sem, 1)])
                        else:
                            p.emit(eng, lambda ee, tt=tt, gidx=gidx, bank=bank: ee.copy(
                                bufX[:, gidx * 4:(gidx + 1) * 4, tt * 128:(tt + 1) * 128],
                                bank[:, :].rearrange("p (j t) -> p j t", j=4)),
                                   waits=[("pe", pe_tr)], incs=[(sem, 1)])
                        S["ps_trX_rd"][gidx % 2] = (sem, p.v(sem))
                    S["xntm_rd_pe"][par2] = p.v("pe")
                    if tt + 3 < BT:
                        emit_load(tt + 3)
                    yield
                S["ln_done"] = [p.v("ve"), p.v("se"), p.v("pe")]
                ln_done = S["ln_done"]
                for grp in range(4):
                    for ki in range(NKD):
                        p.emit("tensor", lambda t, grp=grp, ki=ki: t.matmul(
                            ps_rt[:, :], wrs[:, ki, :], bufX[:, ki, grp * 512:(grp + 1) * 512],
                            start=(ki == 0), stop=(ki == NKD - 1)),
                               waits=([("ve", ln_done[0]), ("se", ln_done[1]), ("gpd", gp_wr),
                                       ("ve", S["rt_ps_free_ve"])] if ki == 0 else []),
                               incs=[("pe", 1)] if ki == NKD - 1 else [])
                    pe_rt = p.v("pe")
                    p.emit("vector", lambda v: v.tensor_copy(rts[0:8, :], ps_rt[:, :]),
                           waits=[("pe", pe_rt), ("pe", S["rts_rd_pe"])], incs=[("ve", 1)])
                    ve_rts = p.v("ve")
                    S["rt_ps_free_ve"] = ve_rts
                    for t4 in range(4):
                        p.emit("tensor", lambda t, t4=t4: t.matmul(
                            ps_rtt[:, t4, :], rts[:, t4 * 128:(t4 + 1) * 128], ident[:, 0:8],
                            start=True, stop=True),
                               waits=([("ve", ve_rts), ("ve", S["rtt_evac_ve"])] if t4 == 0 else []),
                               incs=[("pe", 1)] if t4 == 3 else [])
                    pe_tt = p.v("pe")
                    S["rts_rd_pe"] = pe_tt
                    p.emit("vector", lambda v, grp=grp: v.tensor_copy(
                        rtf32[:, grp * 4:(grp + 1) * 4, :], ps_rtt[:, :, :]),
                           waits=[("pe", pe_tt)], incs=[("ve", 1)])
                    S["rtt_evac_ve"] = p.v("ve")
                    yield
                wp = b % 2
                for tt in range(BT):
                    p.emit("vector", lambda v, tt=tt: v.reduce_max(
                        rsc[:, 0:1], rtf32[:, tt, :], axis=AX.X, negate=True), incs=[("ve", 1)])
                    ve_m = p.v("ve")
                    p.emit("scalar", lambda s, tt=tt: s.activation(Et[:, :], rtf32[:, tt, :], AF.Exp,
                                                                   bias=rsc[:, 0:1]),
                           waits=[("ve", ve_m)], incs=[("se", 1)])
                    se_e = p.v("se")
                    p.emit("vector", lambda v: v.reduce_max(rsc[:, 1:2], Et[:, :], axis=AX.X),
                           waits=[("se", se_e)])
                    p.emit("vector", lambda v: v.tensor_scalar(msk1[:, :], Et[:, :], rsc[:, 1:2],
                                                               None, ALU.is_ge))
                    p.emit("vector", lambda v: v.scalar_tensor_tensor(E2t[:, :], msk1[:, :], -1e30,
                                                                      Et[:, :], ALU.mult, ALU.add))
                    p.emit("vector", lambda v: v.reduce_max(rsc[:, 2:3], E2t[:, :], axis=AX.X))
                    p.emit("vector", lambda v: v.tensor_scalar(msk2[:, :], E2t[:, :], rsc[:, 2:3],
                                                               None, ALU.is_ge))
                    p.emit("vector", lambda v: v.tensor_add(rsc[:, 3:4], rsc[:, 1:2], rsc[:, 2:3]))
                    p.emit("vector", lambda v: v.reciprocal(rsc[:, 3:4], rsc[:, 3:4]))
                    p.emit("vector", lambda v: v.tensor_mul(rsc[:, 4:5], rsc[:, 1:2], rsc[:, 3:4]))
                    p.emit("vector", lambda v: v.tensor_mul(rsc[:, 5:6], rsc[:, 2:3], rsc[:, 3:4]))
                    p.emit("vector", lambda v: v.tensor_scalar(E2t[:, :], msk1[:, :], rsc[:, 4:5],
                                                               None, ALU.mult))
                    p.emit("vector", lambda v, tt=tt, wp=wp: v.scalar_tensor_tensor(
                        wgtb[:, wp, tt, :], msk2[:, :], rsc[:, 5:6], E2t[:, :], ALU.mult, ALU.add),
                           incs=[("ve", 1)])
                    yield
                S["wgtb_wr_ve"] = p.v("ve")

            def emit_h1(b):
                ln_done = S["ln_done_h1"]
                for q in range(4):
                    wp = q % 2
                    p.emit("gpsimd", lambda g, q=q, wp=wp: g.dma_start(
                        out=wbuf[:, wp, :, :],
                        in_=w1T[:, q * 512:(q + 1) * 512].rearrange("(k p) m -> p k m", p=128)),
                           waits=[("pe", S["wbuf_rd_pe"][wp])], incs=[("gpd", 16)])
                    gp_w = p.v("gpd")
                    for mo in range(4):
                        fo = 4 * q + mo
                        for n4 in range(4):
                            hp = (mo * 4 + n4) % 2
                            w0 = [(S["ps_h_rd_sem"][hp], S["ps_h_rd"][hp])]
                            if mo == 0 and n4 == 0:
                                w0.append(("gpd", gp_w))
                                if q == 0:
                                    w0 += [("ve", ln_done[0]), ("se", ln_done[1])]
                            for ki in range(NKD):
                                p.emit("tensor", lambda t, wp=wp, ki=ki, mo=mo, n4=n4, hp=hp: t.matmul(
                                    ps_h[:, hp, :],
                                    wbuf[:, wp, ki, mo * 128:(mo + 1) * 128],
                                    bufX[:, ki, n4 * 512:(n4 + 1) * 512],
                                    start=(ki == 0), stop=(ki == NKD - 1)),
                                       waits=w0 if ki == 0 else [],
                                       incs=[("pe", 1)] if ki == NKD - 1 else [])
                            pe_mm = p.v("pe")
                            S["wbuf_rd_pe"][wp] = pe_mm
                            p.emit("scalar", lambda s, fo=fo, n4=n4, hp=hp: s.activation(
                                bufH[:, fo, n4 * 512:(n4 + 1) * 512],
                                ps_h[:, hp, :], AF.Silu),
                                   waits=[("pe", pe_mm), ("pe", S["bufH_rd_pe"])], incs=[("se", 1)])
                            S["ps_h_rd"][hp] = p.v("se")
                            S["ps_h_rd_sem"][hp] = "se"
                S["bufH_wr_se"] = p.v("se")

            def emit_h2(b, gen=None):
                wpb = b % 2
                wgtb_ve_b = S["wgtb_wr_ve"]
                for q in range(4):
                    wp = q % 2
                    p.emit("gpsimd", lambda g, q=q, wp=wp: g.dma_start(
                        out=wbuf[:, wp, :, :],
                        in_=w3T[:, q * 512:(q + 1) * 512].rearrange("(k p) m -> p k m", p=128)),
                           waits=[("pe", S["wbuf_rd_pe"][wp])], incs=[("gpd", 16)])
                    gp_w = p.v("gpd")

                    def emit_pmm(t_, q=q):
                        w = [("se", S["yb_se"][t_ % 3])]
                        if t_ == 0:
                            w += [("ve", S["psP_free_ve"]), ("ve", wgtb_ve_b)]
                        p.emit("tensor", lambda t, t_=t_: t.matmul(
                            ps_P[:, :], wgtb[:, wpb, t_, :], ybuf[:, t_ % 3, :],
                            start=(t_ == 0), stop=(t_ == BT - 1)),
                               waits=w, incs=[("pe", 1)])
                        S["ybuf_rd_pe"][t_ % 3] = p.v("pe")

                    for tt in range(BT):
                        hp = tt % 2
                        w0 = [(S["ps_h_rd_sem"][hp], S["ps_h_rd"][hp])]
                        if tt == 0:
                            w0.append(("gpd", gp_w))
                            if q == 0:
                                w0.append(("se", S["bufH_wr_se"]))
                        for ki in range(NKD):
                            p.emit("tensor", lambda t, wp=wp, ki=ki, tt=tt, hp=hp: t.matmul(
                                ps_h[:, hp, :], bufH[:, ki, tt * 128:(tt + 1) * 128],
                                wbuf[:, wp, ki, :],
                                start=(ki == 0), stop=(ki == NKD - 1)),
                                   waits=w0 if ki == 0 else [],
                                   incs=[("pe", 1)] if ki == NKD - 1 else [])
                        pe_mm = p.v("pe")
                        S["wbuf_rd_pe"][wp] = pe_mm
                        S["bufH_rd_pe"] = pe_mm
                        p.emit("scalar", lambda s, tt=tt, hp=hp: s.activation(
                            ybuf[:, tt % 3, :], ps_h[:, hp, :], AF.Silu),
                               waits=[("pe", pe_mm), ("pe", S["ybuf_rd_pe"][tt % 3])], incs=[("se", 1)])
                        S["ps_h_rd"][hp] = p.v("se")
                        S["ps_h_rd_sem"][hp] = "se"
                        S["yb_se"][tt % 3] = p.v("se")
                        if gen is not None:
                            next(gen, None)
                        if tt >= 2:
                            emit_pmm(tt - 2)
                    emit_pmm(BT - 2)
                    emit_pmm(BT - 1)
                    pe_pl = p.v("pe")
                    p.emit("vector", lambda v, q=q: v.tensor_add(
                        Pacc[:, q * 512:(q + 1) * 512], Pacc[:, q * 512:(q + 1) * 512],
                        ps_P[:, :]),
                           waits=[("pe", pe_pl)], incs=[("ve", 1)])
                    S["psP_free_ve"] = p.v("ve")
                if gen is not None:
                    for _ in gen:
                        pass

            for _ in emit_prep(0):
                pass
            S["ln_done_h1"] = S["ln_done"]
            for b in range(NBLK):
                emit_h1(b)
                emit_h2(b, gen=emit_prep(b + 1) if b + 1 < NBLK else None)
                S["ln_done_h1"] = S["ln_done"]

            for g in range(4):
                p.emit("vector", lambda v, g=g: v.tensor_copy(
                    rts[0:8, :], Pacc[:, g * 512:(g + 1) * 512]),
                       waits=[("pe", S["rts_rd_pe"])], incs=[("ve", 1)])
                ve_stg = p.v("ve")
                for j in range(4):
                    p.emit("tensor", lambda t, j=j: t.matmul(
                        ps_rtt[:, j, :], rts[:, j * 128:(j + 1) * 128], ident[:, 0:8],
                        start=True, stop=True),
                           waits=([("ve", ve_stg), ("ve", S["rtt_evac_ve"])] if j == 0 else []),
                           incs=[("pe", 1)] if j == 3 else [])
                pe_t = p.v("pe")
                S["rts_rd_pe"] = pe_t
                p.emit("vector", lambda v, g=g: v.tensor_copy(
                    PTb[:, g * 4:(g + 1) * 4, :], ps_rtt[:, :, :]),
                       waits=[("pe", pe_t)], incs=[("ve", 1)])
                S["rtt_evac_ve"] = p.v("ve")
            ptb_ve = p.v("ve")
            sc_ps_free = S["psP_free_ve"]
            for q in range(4):
                wp = q % 2
                p.emit("gpsimd", lambda g, q=q, wp=wp: g.dma_start(
                    out=wbuf[:, wp, :, :],
                    in_=w2T[:, q * 512:(q + 1) * 512].rearrange("(k p) m -> p k m", p=128)),
                       waits=[("pe", S["wbuf_rd_pe"][wp])], incs=[("gpd", 16)])
                gp_w = p.v("gpd")
                for ki in range(NKD):
                    w0 = []
                    if ki == 0:
                        w0 = [("gpd", gp_w), ("ve", ptb_ve), ("ve", sc_ps_free)]
                    p.emit("tensor", lambda t, wp=wp, ki=ki: t.matmul(
                        ps_P[:, :], PTb[:, ki, :], wbuf[:, wp, ki, :],
                        start=(ki == 0), stop=(ki == NKD - 1)),
                           waits=w0, incs=[("pe", 1)] if ki == NKD - 1 else [])
                pe_sc = p.v("pe")
                S["wbuf_rd_pe"][wp] = pe_sc
                p.emit("vector", lambda v, q=q: v.tensor_copy(
                    scatacc[:, q * 512:(q + 1) * 512], ps_P[:, :]),
                       waits=[("pe", pe_sc)], incs=[("ve", 1)])
                sc_ps_free = p.v("ve")

            p.mute = False
            if not use_ffn:
                p.emit("vector", lambda v: v.memset(scatacc[:, :], 0.0), incs=[("ve", 1)])
            scat_final_ve = p.v("ve")

            cfin = xin[0:8, 0, :]
            x8l = xin[0:8, 1, :]
            s8rb = xin[0:8, 2, :]
            p.emit("sync", lambda s: s.dma_start(out=x8l, in_=x_sl[0:8, :]),
                   waits=[("st", S["xin_store"][1]), ("ve", S["xin_rd_ve"][1]),
                          ("se", S["xin_rd_se"][1])],
                   incs=[("ldx", 16)])
            ld_x8l = p.v("ldx")
            if dbg:
                p.emit("sync", lambda s: s.dma_start(out=dbg1, in_=scatacc[:, :]),
                       waits=[("ve", scat_final_ve)], incs=[("st", 16)])
            p.emit("gpsimd", lambda g: g.dma_start(out=ccin[0:8, :], in_=scatacc[:, :]),
                   waits=[("ve", scat_final_ve)], incs=[("gpd", 16)])
            gp_cc = p.v("gpd")
            if use_coll:
                p.emit("gpsimd", lambda g: g.collective_compute(
                    "AllReduce", ALU.add, replica_groups=[list(range(NCORE))],
                    ins=[ccin.ap().opt()], outs=[ccout.ap().opt()]),
                       waits=[("gpd", gp_cc)], incs=[("cc", 1)])
            else:
                p.emit("gpsimd", lambda g: g.dma_start(out=ccout[:, :], in_=ccin[:, :]),
                       waits=[("gpd", gp_cc)], incs=[("cc", 16)])
            p.emit("gpsimd", lambda g: g.dma_start(out=scatacc[:, :], in_=ccout[0:8, :]),
                   waits=[("cc", 1)], incs=[("gpd", 16)])
            p.emit("gpsimd", lambda g: g.dma_start(out=s8rb, in_=ccout[8:16, :]),
                   waits=[("st", S["xin_store"][2]), ("ve", S["xin_rd_ve"][2]),
                          ("se", S["xin_rd_se"][2])],
                   incs=[("gpd", 16)])
            gp_cr = p.v("gpd")
            if dbg:
                p.emit("sync", lambda s: s.dma_start(out=dbg3, in_=ccout[:, :]),
                       waits=[("cc", 1), ("gpd", gp_cr)], incs=[("st", 16)])
            p.emit("scalar", lambda s: s.activation(cfin, scatacc[:, :], AF.Silu),
                   waits=[("gpd", gp_cr), ("st", S["xin_store"][0]),
                          ("ve", S["xin_rd_ve"][0]),
                          ("se", S["xin_rd_se"][0])], incs=[("se", 1)])
            se_g8 = p.v("se")
            p.emit("vector", lambda v: v.tensor_mul(cfin, cfin, s8rb),
                   waits=[("se", se_g8), ("gpd", gp_cr)])
            p.emit("vector", lambda v: v.tensor_scalar(cfin, cfin, mask8[:, :], None,
                                                       ALU.mult),
                   waits=[("ldx", const_ld)])
            p.emit("vector", lambda v: v.tensor_add(cfin, cfin, x8l),
                   waits=[("ldx", ld_x8l)], incs=[("ve", 1)])
            ve_o8 = p.v("ve")
            p.emit("sync", lambda s: s.dma_start(out=out[0:8, :], in_=cfin),
                   waits=[("ve", ve_o8)], incs=[("st", 16)])
            p.emit("sync", lambda s: s.wait_ge(p.hand["st"], p.v("st")))


            block = root.enter_context(nc.Block())
            p.finalize(block)
    return nc


_NC_CACHE = None


def _get_nc():
    global _NC_CACHE
    if _NC_CACHE is None:
        _NC_CACHE = build_program()
    return _NC_CACHE


def prep_in_maps(L0_attn_hidden_states, L0_attn_W_qkv, L0_attn_W_z, L0_attn_W_a,
                 L0_attn_W_b, L0_attn_conv_w, L0_attn_A_log, L0_attn_dt_bias,
                 L0_attn_norm_w, L0_attn_W_out, L0_ffn_w1, L0_ffn_w3, L0_ffn_w2,
                 L0_ffn_wr):
    f32 = np.float32
    x = np.ascontiguousarray(np.asarray(L0_attn_hidden_states, dtype=f32)).reshape(TOK, D)
    Wqkv = np.asarray(L0_attn_W_qkv, dtype=f32)
    Wz = np.asarray(L0_attn_W_z, dtype=f32)
    Wa = np.asarray(L0_attn_W_a, dtype=f32)
    Wb = np.asarray(L0_attn_W_b, dtype=f32)
    cwf = np.asarray(L0_attn_conv_w, dtype=f32)
    Alog = np.asarray(L0_attn_A_log, dtype=f32)
    dtb = np.asarray(L0_attn_dt_bias, dtype=f32)
    nw = np.asarray(L0_attn_norm_w, dtype=f32)
    Wout = np.asarray(L0_attn_W_out, dtype=f32)
    w1 = np.asarray(L0_ffn_w1, dtype=f32)
    w3 = np.asarray(L0_ffn_w3, dtype=f32)
    w2 = np.asarray(L0_ffn_w2, dtype=f32)
    wr = np.asarray(L0_ffn_wr, dtype=f32)

    w1T = np.ascontiguousarray(w1.T)
    w3T = np.ascontiguousarray(w3.T)
    w2T = np.ascontiguousarray(w2.T)
    wrT = np.ascontiguousarray(wr.T)
    WoutT = np.ascontiguousarray(Wout.T)
    ones = np.ones((128, 128), f32)
    ident = np.eye(128, dtype=f32)
    identb = np.eye(128).astype(ml_dtypes.bfloat16)
    x8 = np.ascontiguousarray(x[0:8, :])

    in_maps = []
    for c in range(NCORE):
        kh0 = 2 * c * 128
        vh0 = 4 * c * 128
        in_maps.append({
            "x_sl": np.ascontiguousarray(x[c * TOKC:(c + 1) * TOKC]),
            "x8": x8,
            "w1T": w1T, "w3T": w3T, "w2T": w2T, "wrT": wrT,
            "wqT": np.ascontiguousarray(Wqkv[kh0:kh0 + 256, :].T),
            "wkT": np.ascontiguousarray(Wqkv[KEY + kh0:KEY + kh0 + 256, :].T),
            "wvT": np.ascontiguousarray(Wqkv[2 * KEY + vh0:2 * KEY + vh0 + 512, :].T),
            "wzT": np.ascontiguousarray(Wz[vh0:vh0 + 512, :].T),
            "wabT": np.ascontiguousarray(
                np.concatenate([Wa[4 * c:4 * c + 4], Wb[4 * c:4 * c + 4]], 0).T),
            "woT": np.ascontiguousarray(WoutT[vh0:vh0 + 512, :]),
            "cw": np.ascontiguousarray(np.concatenate(
                [cwf[kh0:kh0 + 256], cwf[KEY + kh0:KEY + kh0 + 256],
                 cwf[2 * KEY + vh0:2 * KEY + vh0 + 512]], 0)),
            "alog": np.ascontiguousarray(Alog[4 * c:4 * c + 4].reshape(4, 1)),
            "dtb": np.ascontiguousarray(dtb[4 * c:4 * c + 4].reshape(4, 1)),
            "normw": np.ascontiguousarray(nw.reshape(128, 1)),
            "mask8": np.full((8, 1), 1.0 if c == 0 else 0.0, f32),
            "ones": ones, "ident": ident, "identb": identb,
        })
    return in_maps


def kernel(**inputs):
    in_maps = prep_in_maps(**inputs)
    nc = _get_nc()
    res = run_bass_kernel_spmd(nc, in_maps, core_ids=list(range(NCORE)))
    outp = np.concatenate([res.results[c]["out"] for c in range(NCORE)], axis=0)
    return outp.reshape(B, S, D)




# version 16 (final)
# speedup vs baseline: 1.1001x; 1.0040x over previous
import numpy as np
import ml_dtypes
from contextlib import ExitStack

import concourse.bass as bass
import concourse.mybir as mybir
from concourse.bass_utils import run_bass_kernel_spmd

F32 = mybir.dt.float32
BF16 = mybir.dt.bfloat16
AF = mybir.ActivationFunctionType
ALU = mybir.AluOpType
AX = mybir.AxisListType

NCORE = 8
B, S, D = 4, 8192, 2048
TOK = B * S
TOKC = TOK // NCORE
NT = TOKC // 128
NBLK = 2
BT = NT // NBLK
NKD = D // 128
HV, HK, DK, DV = 32, 16, 128, 128
KEY, VAL = HK * DK, HV * DV
E8 = 8
RD = 1.0 / D
EPS_LN = 1e-5
EPS_AT = 1e-6
DKI = DK ** (-0.5)

ENGS = ("sync", "gpsimd", "vector", "scalar", "tensor")


class Prog:

    def __init__(self, nc):
        self.nc = nc
        self.ops = {e: [] for e in ENGS}
        self.cnt = {}
        self.hand = {}
        self.mute = False

    def sem(self, ctx, name):
        self.hand[name] = ctx.enter_context(self.nc.semaphore(name))
        self.cnt[name] = 0

    def emit(self, eng, builder, waits=(), incs=()):
        if self.mute:
            return
        rw = [(self.hand[nm], int(v)) for nm, v in waits]
        ri = []
        for nm, n in incs:
            self.cnt[nm] += n
            ri.append((self.hand[nm], n, self.cnt[nm]))
        self.ops[eng].append((builder, rw, ri))

    def v(self, nm):
        return self.cnt[nm]

    def finalize(self, block):
        self_sync = {"vector": "vq", "scalar": "sq"}
        for eng in ENGS:
            ops = self.ops[eng]
            if not ops:
                continue

            def body(e, ops=ops, eng=eng):
                sq = self.hand.get(self_sync.get(eng))
                sq_cnt = 0
                last_edge = None
                for builder, waits, incs in ops:
                    for h, val in waits:
                        if val > 0:
                            e.wait_ge(h, val)
                    if sq is not None and last_edge is not None:
                        e.wait_ge(last_edge[0], last_edge[1])
                    ins = builder(e)
                    for h, n, post in incs:
                        if ins is None:
                            e.sem_inc(h, n)
                        else:
                            ins = ins.then_inc(h, n)
                    if sq is not None:
                        if ins is None:
                            if incs:
                                pass
                        elif incs:
                            last_edge = (incs[0][0], incs[0][2])
                        else:
                            sq_cnt += 1
                            ins.then_inc(sq, 1)
                            last_edge = (sq, sq_cnt)
            getattr(block, eng)(body)


def build_program(use_mini=True, use_ffn=True, use_coll=True, dbg=False):
    mlvl = 7 if use_mini is True else (0 if use_mini is False else int(use_mini))
    nc = bass.Bass("TRN2", target_bir_lowering=False, detect_race_conditions=False)

    x_sl = nc.dram_tensor("x_sl", [TOKC, D], F32, kind="ExternalInput").ap()
    x8 = nc.dram_tensor("x8", [8, D], F32, kind="ExternalInput").ap()
    w1T = nc.dram_tensor("w1T", [D, D], F32, kind="ExternalInput").ap()
    w3T = nc.dram_tensor("w3T", [D, D], F32, kind="ExternalInput").ap()
    w2T = nc.dram_tensor("w2T", [D, D], F32, kind="ExternalInput").ap()
    wrT = nc.dram_tensor("wrT", [D, E8], F32, kind="ExternalInput").ap()
    wqT = nc.dram_tensor("wqT", [D, 256], F32, kind="ExternalInput").ap()
    wkT = nc.dram_tensor("wkT", [D, 256], F32, kind="ExternalInput").ap()
    wvT = nc.dram_tensor("wvT", [D, 512], F32, kind="ExternalInput").ap()
    wzT = nc.dram_tensor("wzT", [D, 512], F32, kind="ExternalInput").ap()
    wabT = nc.dram_tensor("wabT", [D, 8], F32, kind="ExternalInput").ap()
    woT = nc.dram_tensor("woT", [512, D], F32, kind="ExternalInput").ap()
    cwI = nc.dram_tensor("cw", [1024, 4], F32, kind="ExternalInput").ap()
    alogI = nc.dram_tensor("alog", [4, 1], F32, kind="ExternalInput").ap()
    dtbI = nc.dram_tensor("dtb", [4, 1], F32, kind="ExternalInput").ap()
    normwI = nc.dram_tensor("normw", [128, 1], F32, kind="ExternalInput").ap()
    mask8I = nc.dram_tensor("mask8", [8, 1], F32, kind="ExternalInput").ap()
    onesI = nc.dram_tensor("ones", [128, 128], F32, kind="ExternalInput").ap()
    identI = nc.dram_tensor("ident", [128, 128], F32, kind="ExternalInput").ap()
    identbI = nc.dram_tensor("identb", [128, 128], BF16, kind="ExternalInput").ap()
    out = nc.dram_tensor("out", [TOKC, D], F32, kind="ExternalOutput").ap()

    if dbg:
        dbg1 = nc.dram_tensor("dbg1", [8, D], F32, kind="ExternalOutput").ap()
        dbg2 = nc.dram_tensor("dbg2", [8, D], F32, kind="ExternalOutput").ap()
        dbg3 = nc.dram_tensor("dbg3", [16, D], F32, kind="ExternalOutput").ap()
    ccin = nc.dram_tensor("ccin", [16, D], F32)
    ccout = nc.dram_tensor("ccout", [16, D], F32)

    with ExitStack() as root:
        e = root.enter_context
        ones = e(nc.sbuf_tensor("ones_sb", [128, 128], F32))
        ident = e(nc.sbuf_tensor("ident_sb", [128, 128], F32))
        identb = e(nc.sbuf_tensor("identb_sb", [128, 128], BF16))
        mask8 = e(nc.sbuf_tensor("mask8_sb", [8, 1], F32))

        cst = e(nc.sbuf_tensor("cst", [128, 4], F32))

        p = Prog(nc)
        for s_ in ("ldx", "gpd", "pe", "ve", "se", "st", "cc", "vq", "sq"):
            p.sem(root, s_)

        p.emit("vector", lambda v: v.memset(cst[:, 0:1], EPS_LN))
        p.emit("vector", lambda v: v.memset(cst[:, 1:2], EPS_AT))
        p.emit("vector", lambda v: v.memset(cst[:, 2:3], 1.0), incs=[("ve", 1)])

        p.emit("sync", lambda s: s.dma_start(out=ones[:, :], in_=onesI), incs=[("ldx", 16)])
        p.emit("sync", lambda s: s.dma_start(out=ident[:, :], in_=identI), incs=[("ldx", 16)])
        p.emit("sync", lambda s: s.dma_start(out=identb[:, :], in_=identbI), incs=[("ldx", 16)])
        p.emit("sync", lambda s: s.dma_start(out=mask8[:, :], in_=mask8I), incs=[("ldx", 16)])
        const_ld = p.v("ldx")

        p.mute = mlvl < 1
        with ExitStack() as ma:
            em = ma.enter_context
            wq = em(nc.sbuf_tensor("wq", [128, NKD, 256], F32))
            wk = em(nc.sbuf_tensor("wk", [128, NKD, 256], F32))
            wv = em(nc.sbuf_tensor("wv", [128, NKD, 512], F32))
            wz = em(nc.sbuf_tensor("wz", [128, NKD, 512], F32))
            wab = em(nc.sbuf_tensor("wab", [128, NKD, 8], F32))
            wo = em(nc.sbuf_tensor("wo", [128, 4, D], F32))
            cw = em(nc.sbuf_tensor("cw_sb", [128, 8, 4], F32))
            alog = em(nc.sbuf_tensor("alog_sb", [4, 1], F32))
            dtb = em(nc.sbuf_tensor("dtb_sb", [4, 1], F32))
            normw = em(nc.sbuf_tensor("normw_sb", [128, 1], F32))
            x8s = em(nc.sbuf_tensor("x8s", [8, D], F32))
            xn8 = em(nc.sbuf_tensor("xn8", [128, D], F32))
            st8 = em(nc.sbuf_tensor("st8", [8, 8], F32))
            xn8fm = em(nc.sbuf_tensor("xn8fm", [128, NKD, 8], F32))
            rawp = em(nc.sbuf_tensor("rawp", [128, 8, 11], F32))
            cvt = em(nc.sbuf_tensor("cvt", [128, 8, 8], F32))
            mx = em(nc.sbuf_tensor("mx", [128, 8, 8], F32))
            zt = em(nc.sbuf_tensor("zt", [128, 4, 8], F32))
            absb = em(nc.sbuf_tensor("absb", [4, 16], F32))
            sqs = em(nc.sbuf_tensor("sqs", [128, 8], F32))
            nr = em(nc.sbuf_tensor("nr", [128, 32], F32))
            qn = em(nc.sbuf_tensor("qn", [128, 4, 8], F32))
            a2 = em(nc.sbuf_tensor("a2", [4, 8], F32))
            spx = em(nc.sbuf_tensor("spx", [4, 8], F32))
            spl = em(nc.sbuf_tensor("spl", [4, 8], F32))
            spm = em(nc.sbuf_tensor("spm", [4, 8], F32))
            gd = em(nc.sbuf_tensor("gd", [4, 8], F32))
            Es = em(nc.sbuf_tensor("Es", [4, 8], F32))
            Bs = em(nc.sbuf_tensor("Bs", [128, 8], F32))
            eA = em(nc.sbuf_tensor("eA", [4, 1], F32))
            bT = em(nc.sbuf_tensor("bT", [8, 4], F32))
            erow = em(nc.sbuf_tensor("erow", [128, 32], F32))
            Ebc = em(nc.sbuf_tensor("Ebc", [128, 4, 8], F32))
            ktr = em(nc.sbuf_tensor("ktr", [8, 128], F32))
            vtr = em(nc.sbuf_tensor("vtr", [8, 128], F32))
            kbt = em(nc.sbuf_tensor("kbt", [8, 128], F32))
            krow = em(nc.sbuf_tensor("krow", [128, 1024], F32))
            vrow = em(nc.sbuf_tensor("vrow", [1, 1024], F32))
            dls = em(nc.sbuf_tensor("dls", [128, 128], F32))
            Sst = em(nc.sbuf_tensor("Sst", [128, 128], F32))
            Oall = em(nc.sbuf_tensor("Oall", [128, 4, 8], F32))
            Ofin = em(nc.sbuf_tensor("Ofin", [128, 4, 8], F32))
            szal = em(nc.sbuf_tensor("szal", [128, 4, 8], F32))
            s8stg = em(nc.sbuf_tensor("s8stg", [8, 512], F32))

            def gld(dst, src):
                p.emit("gpsimd", lambda g, d=dst, s=src: g.dma_start(out=d, in_=s),
                       incs=[("gpd", 16)])
            gld(x8s[:, :], x8)
            gld(cw[:, :, :], cwI.rearrange("(c p) k -> p c k", p=128))
            gld(alog[:, :], alogI)
            gld(dtb[:, :], dtbI)
            gld(normw[:, :], normwI)
            mini_g1 = p.v("gpd")
            gld(wq[:, :, :], wqT.rearrange("(k p) m -> p k m", p=128))
            gld(wk[:, :, :], wkT.rearrange("(k p) m -> p k m", p=128))
            gld(wab[:, :, :], wabT.rearrange("(k p) m -> p k m", p=128))
            mini_g2 = p.v("gpd")
            gld(wv[:, :, :], wvT.rearrange("(k p) m -> p k m", p=128))
            gld(wz[:, :, :], wzT.rearrange("(k p) m -> p k m", p=128))
            mini_g3 = p.v("gpd")
            mini_ld = p.v("gpd")
            p.emit("sync", lambda s: s.dma_start(out=wo[:, :, :],
                                                 in_=woT.rearrange("(h p) m -> p h m", p=128)),
                   incs=[("ldx", 16)])
            ldx_wo = p.v("ldx")

            p.emit("vector", lambda v: v.memset(xn8[:, :], 0.0))
            p.emit("vector", lambda v: v.memset(Bs[:, :], 0.0))
            p.emit("vector", lambda v: v.memset(nr[:, :], 0.0))
            p.emit("vector", lambda v: v.memset(erow[:, :], 0.0))
            p.emit("vector", lambda v: v.memset(krow[:, :], 0.0))
            p.emit("vector", lambda v: v.memset(dls[:, :], 0.0), incs=[("ve", 1)])

            with ExitStack() as mp1:
                e1 = mp1.enter_context
                ps_t8 = e1(nc.psum_tensor("ps_t8", [128, NKD, 8], F32))
                ps_p = e1(nc.psum_tensor("ps_p", [128, 8], F32))
                ps_p2 = e1(nc.psum_tensor("ps_p2", [128, 8], F32))
                ps_ab = e1(nc.psum_tensor("ps_ab", [4, 16], F32))
                ps_n = e1(nc.psum_tensor("ps_n", [1, 64], F32))
                ps_b = e1(nc.psum_tensor("ps_b", [128, 64], F32))
                ps_t2 = e1(nc.psum_tensor("ps_t2", [8, 128], F32))

                p.emit("vector", lambda v: v.reduce_sum(st8[:, 0:1], x8s[:, :], axis=AX.X, negate=True),
                       waits=[("gpd", mini_g1)])
                p.emit("vector", lambda v: v.tensor_scalar_mul(st8[:, 1:2], st8[:, 0:1], RD))
                ve_nm = p.v("ve")
                p.emit("vector", lambda v: None, incs=[("ve", 1)])
                p.emit("scalar", lambda s: s.activation(xn8[0:8, :], x8s[:, :], AF.Square,
                                                        bias=st8[:, 1:2], scale=1.0,
                                                        accum_out=st8[:, 2:3]),
                       waits=[("ve", p.v("ve"))])
                p.emit("scalar", lambda s: s.activation(st8[:, 3:4], st8[:, 2:3], AF.Sqrt,
                                                        bias=cst[0:8, 0:1], scale=RD),
                       incs=[("se", 1)])
                se_sv = p.v("se")
                p.emit("vector", lambda v: v.reciprocal(st8[:, 4:5], st8[:, 3:4]),
                       waits=[("se", se_sv)])
                p.emit("vector", lambda v: v.tensor_scalar(xn8[0:8, :], x8s[:, :], st8[:, 1:2],
                                                           st8[:, 4:5], ALU.add, ALU.mult),
                       incs=[("ve", 1)])
                ve_xn8 = p.v("ve")

                p.mute = mlvl < 2
                for j in range(NKD):
                    p.emit("tensor", lambda t, j=j: t.matmul(ps_t8[:, j, :], xn8[:, j * 128:(j + 1) * 128],
                                                             ident[:, 0:8], start=True, stop=True),
                           waits=[("ve", ve_xn8), ("ldx", const_ld)] if j == 0 else [],
                           incs=[("pe", 1)] if j == NKD - 1 else [])
                pe_t8 = p.v("pe")
                p.emit("vector", lambda v: v.tensor_copy(xn8fm[:, :, :], ps_t8[:, :, :]),
                       waits=[("pe", pe_t8)], incs=[("ve", 1)])
                ve_fm = p.v("ve")

                p.emit("vector", lambda v: v.memset(rawp[:, :, :], 0.0), incs=[("ve", 1)])
                ve_pad = p.v("ve")

                proj = []
                for mo in range(2):
                    proj.append((wq, mo, ("rawp", mo)))
                for mo in range(2):
                    proj.append((wk, mo, ("rawp", 2 + mo)))
                for mo in range(4):
                    proj.append((wv, mo, ("rawp", 4 + mo)))
                for mo in range(4):
                    proj.append((wz, mo, ("zt", mo)))
                pe_evac = {}
                for i, (wt, mo, dst) in enumerate(proj):
                    par = i % 2
                    ps = ps_p if par == 0 else ps_p2
                    w8 = []
                    if i == 0:
                        w8.append(("ve", ve_fm))
                        w8.append(("gpd", mini_g2))
                    if i == 4:
                        w8.append(("gpd", mini_g3))
                    if par in pe_evac:
                        w8.append(("ve", pe_evac[par]))
                    for ki in range(NKD):
                        p.emit("tensor",
                               lambda t, wt=wt, mo=mo, ki=ki, ps=ps: t.matmul(
                                   ps[:, :], wt[:, ki, mo * 128:(mo + 1) * 128], xn8fm[:, ki, :],
                                   start=(ki == 0), stop=(ki == NKD - 1)),
                               waits=w8 if ki == 0 else [],
                               incs=[("pe", 1)] if ki == NKD - 1 else [])
                    pe_mm = p.v("pe")
                    kind, ch = dst
                    if kind == "rawp":
                        p.emit("vector", lambda v, ch=ch, ps=ps: v.tensor_copy(rawp[:, ch, 3:11], ps[:, :]),
                               waits=[("pe", pe_mm), ("ve", ve_pad)], incs=[("ve", 1)])
                    else:
                        p.emit("vector", lambda v, ch=ch, ps=ps: v.tensor_copy(zt[:, ch, :], ps[:, :]),
                               waits=[("pe", pe_mm)], incs=[("ve", 1)])
                    pe_evac[par] = p.v("ve")
                for half in range(2):
                    for ki in range(NKD):
                        p.emit("tensor",
                               lambda t, ki=ki, half=half: t.matmul(
                                   ps_ab[:, 8 * half:8 * half + 8],
                                   wab[:, ki, 4 * half:4 * half + 4], xn8fm[:, ki, :],
                                   start=(ki == 0), stop=(ki == NKD - 1)),
                               waits=[("gpd", mini_g2)] if (half == 0 and ki == 0) else [],
                               incs=[("pe", 1)] if ki == NKD - 1 else [])
                pe_ab = p.v("pe")
                p.emit("vector", lambda v: v.tensor_copy(absb[:, :], ps_ab[:, :]),
                       waits=[("pe", pe_ab)], incs=[("ve", 1)])
                ve_ab = p.v("ve")

                p.mute = mlvl < 3
                for ch in range(8):
                    p.emit("vector", lambda v, ch=ch: v.tensor_scalar(
                        cvt[:, ch, :], rawp[:, ch, 0:8], cw[:, ch, 0:1], None, ALU.mult))
                    for k in range(1, 4):
                        p.emit("vector", lambda v, ch=ch, k=k: v.scalar_tensor_tensor(
                            cvt[:, ch, :], rawp[:, ch, k:k + 8], cw[:, ch, k:k + 1],
                            cvt[:, ch, :], ALU.mult, ALU.add))
                p.emit("vector", lambda v: None, incs=[("ve", 1)])
                ve_cvt = p.v("ve")
                p.emit("scalar", lambda s: s.activation(mx[:, :, :], cvt[:, :, :], AF.Silu),
                       waits=[("ve", ve_cvt)], incs=[("se", 1)])
                se_mx = p.v("se")

                for ch in range(4):
                    p.emit("scalar", lambda s, ch=ch: s.activation(sqs[:, :], mx[:, ch, :], AF.Square),
                           waits=[("se", se_mx)], incs=[("se", 1)])
                    se_sq = p.v("se")
                    p.emit("tensor", lambda t, ch=ch: t.matmul(ps_n[0:1, 8 * ch:8 * ch + 8],
                                                               ones[:, 0:1], sqs[:, :],
                                                               start=True, stop=True),
                           waits=[("se", se_sq)], incs=[("pe", 1)])
                    pe_n = p.v("pe")
                    p.emit("scalar", lambda s, ch=ch: s.activation(nr[0:1, 8 * ch:8 * ch + 8],
                                                                   ps_n[0:1, 8 * ch:8 * ch + 8], AF.Sqrt),
                           waits=[("pe", pe_n)], incs=[("se", 1)])
                    se_nr = p.v("se")
                    p.emit("vector", lambda v, ch=ch: v.tensor_scalar_max(
                        nr[0:1, 8 * ch:8 * ch + 8], nr[0:1, 8 * ch:8 * ch + 8], 1e-6),
                           waits=[("se", se_nr)])
                    p.emit("vector", lambda v, ch=ch: v.reciprocal(
                        nr[0:1, 8 * ch:8 * ch + 8], nr[0:1, 8 * ch:8 * ch + 8]))
                    if ch < 2:
                        p.emit("vector", lambda v, ch=ch: v.tensor_scalar_mul(
                            nr[0:1, 8 * ch:8 * ch + 8], nr[0:1, 8 * ch:8 * ch + 8], DKI))
                    p.emit("vector", lambda v: None, incs=[("ve", 1)])
                    ve_r = p.v("ve")
                    p.emit("tensor", lambda t, ch=ch: t.matmul(ps_b[:, 8 * ch:8 * ch + 8],
                                                               ones[:, :], nr[:, 8 * ch:8 * ch + 8],
                                                               start=True, stop=True),
                           waits=[("ve", ve_r)], incs=[("pe", 1)])
                    pe_b = p.v("pe")
                    p.emit("vector", lambda v, ch=ch: v.tensor_mul(qn[:, ch, :], mx[:, ch, :],
                                                                   ps_b[:, 8 * ch:8 * ch + 8]),
                           waits=[("pe", pe_b)], incs=[("ve", 1)])
                ve_qn = p.v("ve")

                p.mute = mlvl < 4
                p.emit("vector", lambda v: v.tensor_scalar_add(a2[:, :], absb[0:4, 0:8], dtb[:, :]),
                       waits=[("ve", ve_ab)], incs=[("ve", 1)])
                ve_a2 = p.v("ve")
                p.emit("scalar", lambda s: s.activation(spx[:, :], a2[:, :], AF.Abs),
                       waits=[("ve", ve_a2)])
                p.emit("scalar", lambda s: s.activation(spx[:, :], spx[:, :], AF.Exp, scale=-1.0))
                p.emit("scalar", lambda s: s.activation(spl[:, :], spx[:, :], AF.Ln, bias=cst[0:4, 2:3]),
                       incs=[("se", 1)])
                se_spl = p.v("se")
                p.emit("vector", lambda v: v.tensor_scalar_max(spm[:, :], a2[:, :], 0.0))
                p.emit("vector", lambda v: v.tensor_add(spm[:, :], spm[:, :], spl[:, :]),
                       waits=[("se", se_spl)], incs=[("ve", 1)])
                ve_sp = p.v("se")
                p.emit("scalar", lambda s: s.activation(eA[:, :], alog[:, :], AF.Exp),
                       incs=[("se", 1)])
                se_eA = p.v("se")
                p.emit("vector", lambda v: v.tensor_scalar_mul(eA[:, :], eA[:, :], -1.0),
                       waits=[("se", se_eA)])
                p.emit("vector", lambda v: v.tensor_scalar(gd[:, :], spm[:, :], eA[:, :], None, ALU.mult),
                       incs=[("ve", 1)])
                ve_gd = p.v("ve")
                p.emit("scalar", lambda s: s.activation(Es[:, :], gd[:, :], AF.Exp),
                       waits=[("ve", ve_gd)])
                p.emit("scalar", lambda s: s.activation(Bs[0:4, :], absb[0:4, 8:16], AF.Sigmoid),
                       incs=[("se", 1)])
                se_EB = p.v("se")

                p.emit("tensor", lambda t: t.matmul(ps_t2[0:8, 0:4], Bs[:, :], ident[:, 0:4],
                                                    start=True, stop=True),
                       waits=[("se", se_EB)], incs=[("pe", 1)])
                pe_bT = p.v("pe")
                p.emit("vector", lambda v: v.tensor_copy(bT[:, :], ps_t2[0:8, 0:4]),
                       waits=[("pe", pe_bT)], incs=[("ve", 1)])
                ve_bT = p.v("ve")
                p.emit("gpsimd", lambda g: g.dma_start(out=erow[0:1, :], in_=Es[:, :]),
                       waits=[("se", se_EB)], incs=[("gpd", 16)])
                gp_er = p.v("gpd")

                ve_ebc = 0
                for h in range(4):
                    p.emit("tensor", lambda t, h=h: t.matmul(ps_b[:, 8 * h:8 * h + 8], ones[:, :],
                                                             erow[:, 8 * h:8 * h + 8],
                                                             start=True, stop=True),
                           waits=([("gpd", gp_er), ("ve", ve_qn)] if h == 0
                                  else [("ve", ve_ebc)]),
                           incs=[("pe", 1)])
                    pe_eb = p.v("pe")
                    p.emit("vector", lambda v, h=h: v.tensor_copy(Ebc[:, h, :], ps_b[:, 8 * h:8 * h + 8]),
                           waits=[("pe", pe_eb)], incs=[("ve", 1)])
                    ve_ebc = p.v("ve")

            with ExitStack() as mp2:
                e2 = mp2.enter_context
                ps_t2b = e2(nc.psum_tensor("ps_t2b", [8, 128], F32))
                ps_kv = e2(nc.psum_tensor("ps_kv", [1, 128], F32))
                ps_ou = e2(nc.psum_tensor("ps_ou", [128, 128], F32))
                ps_oc = e2(nc.psum_tensor("ps_oc", [128, 1], F32))
                ps_n2 = e2(nc.psum_tensor("ps_n2", [1, 32], F32))
                ps_b2 = e2(nc.psum_tensor("ps_b2", [128, 32], F32))
                ps_s8 = e2(nc.psum_tensor("ps_s8", [8, 512], F32))

                p.mute = mlvl < 5
                pe_head_end = 0
                ve_dl = 0
                ve_vt = ve_bT
                for h in range(4):
                    kh = h // 2
                    p.emit("tensor", lambda t, kh=kh: t.matmul(ps_t2b[:, :], qn[:, 2 + kh, :],
                                                               ident[:, :], start=True, stop=True),
                           waits=[("ve", ve_bT), ("ve", ve_vt)], incs=[("pe", 1)])
                    pe_kt = p.v("pe")
                    p.emit("vector", lambda v: v.tensor_copy(ktr[:, :], ps_t2b[:, :]),
                           waits=[("pe", pe_kt), ("gpd", p.v("gpd"))], incs=[("ve", 1)])
                    ve_kt = p.v("ve")
                    p.emit("vector", lambda v, h=h: v.tensor_scalar(kbt[:, :], ktr[:, :],
                                                                    bT[:, h:h + 1], None, ALU.mult),
                           incs=[("ve", 1)])
                    ve_kb = p.v("ve")
                    p.emit("gpsimd", lambda g: g.dma_start(out=krow[0:1, :], in_=kbt[:, :]),
                           waits=[("ve", ve_kb), ("pe", pe_head_end)], incs=[("gpd", 16)])
                    p.emit("tensor", lambda t, h=h: t.matmul(ps_t2b[:, :], mx[:, 4 + h, :],
                                                             ident[:, :], start=True, stop=True),
                           waits=[("ve", ve_kt)], incs=[("pe", 1)])
                    pe_vt = p.v("pe")
                    p.emit("vector", lambda v: v.tensor_copy(vtr[:, :], ps_t2b[:, :]),
                           waits=[("pe", pe_vt)], incs=[("ve", 1)])
                    ve_vt = p.v("ve")
                    p.emit("gpsimd", lambda g: g.dma_start(out=vrow[:, :], in_=vtr[:, :]),
                           waits=[("ve", ve_vt), ("ve", ve_dl)], incs=[("gpd", 16)])
                    gp_head = p.v("gpd")
                    p.emit("vector", lambda v: v.memset(Sst[:, :], 0.0),
                           waits=[("gpd", gp_head), ("pe", pe_head_end)], incs=[("ve", 1)])
                    for t_ in range(8):
                        ve_S = p.v("ve")
                        p.emit("vector", lambda v, h=h, t_=t_: v.tensor_scalar_mul(
                            Sst[:, :], Sst[:, :], Ebc[:, h, t_:t_ + 1]), incs=[("ve", 1)])
                        ve_dec = p.v("ve")
                        p.emit("tensor", lambda t, kh=kh, t_=t_: t.matmul(
                            ps_kv[:, :], qn[:, 2 + kh, t_:t_ + 1], Sst[:, :], start=True, stop=True),
                               waits=[("ve", ve_dec)], incs=[("pe", 1)])
                        pe_kv = p.v("pe")
                        p.emit("vector", lambda v, t_=t_: v.scalar_tensor_tensor(
                            dls[0:1, :], ps_kv[:, :], -1.0, vrow[0:1, 128 * t_:128 * (t_ + 1)],
                            ALU.mult, ALU.add),
                               waits=[("pe", pe_kv)], incs=[("ve", 1)])
                        ve_dl = p.v("ve")
                        p.emit("tensor", lambda t, t_=t_: t.matmul(
                            ps_ou[:, :], krow[:, 128 * t_:128 * (t_ + 1)],
                            dls[:, :], start=True, stop=True),
                               waits=[("ve", ve_dl)], incs=[("pe", 1)])
                        pe_ou = p.v("pe")
                        p.emit("vector", lambda v: v.tensor_add(Sst[:, :], Sst[:, :], ps_ou[:, :]),
                               waits=[("pe", pe_ou)], incs=[("ve", 1)])
                        ve_Su = p.v("ve")
                        p.emit("tensor", lambda t, kh=kh, t_=t_: t.matmul(
                            ps_oc[:, :], Sst[:, :], qn[:, kh, t_:t_ + 1], start=True, stop=True),
                               waits=[("ve", ve_Su)], incs=[("pe", 1)])
                        pe_oc = p.v("pe")
                        p.emit("vector", lambda v, h=h, t_=t_: v.tensor_copy(
                            Oall[:, h, t_:t_ + 1], ps_oc[:, :]),
                               waits=[("pe", pe_oc)], incs=[("ve", 1)])
                    pe_head_end = p.v("pe")
                ve_rec = p.v("ve")

                p.mute = mlvl < 6
                p.emit("scalar", lambda s: s.activation(szal[:, :, :], zt[:, :, :], AF.Silu),
                       waits=[("ve", pe_evac.get(0, 0)), ("ve", pe_evac.get(1, 0))], incs=[("se", 1)])
                for h in range(4):
                    p.emit("scalar", lambda s, h=h: s.activation(sqs[:, :], Oall[:, h, :], AF.Square),
                           waits=[("ve", ve_rec)], incs=[("se", 1)])
                    se_sq2 = p.v("se")
                    p.emit("tensor", lambda t, h=h: t.matmul(ps_n2[0:1, 8 * h:8 * h + 8], ones[:, 0:1],
                                                             sqs[:, :], start=True, stop=True),
                           waits=[("se", se_sq2)], incs=[("pe", 1)])
                    pe_n2 = p.v("pe")
                    p.emit("scalar", lambda s, h=h: s.activation(nr[0:1, 8 * h:8 * h + 8],
                                                                 ps_n2[0:1, 8 * h:8 * h + 8], AF.Sqrt,
                                                                 scale=1.0 / DV, bias=cst[0:1, 1:2]),
                           waits=[("pe", pe_n2)], incs=[("se", 1)])
                    se_nr2 = p.v("se")
                    p.emit("vector", lambda v, h=h: v.reciprocal(nr[0:1, 8 * h:8 * h + 8],
                                                                 nr[0:1, 8 * h:8 * h + 8]),
                           waits=[("se", se_nr2)], incs=[("ve", 1)])
                    ve_r2 = p.v("ve")
                    p.emit("tensor", lambda t, h=h: t.matmul(ps_b2[:, 8 * h:8 * h + 8], ones[:, :],
                                                             nr[:, 8 * h:8 * h + 8], start=True, stop=True),
                           waits=[("ve", ve_r2)], incs=[("pe", 1)])
                    pe_b2 = p.v("pe")
                    p.emit("vector", lambda v, h=h: v.scalar_tensor_tensor(
                        Ofin[:, h, :], Oall[:, h, :], normw[:, :], ps_b2[:, 8 * h:8 * h + 8],
                        ALU.mult, ALU.mult),
                           waits=[("pe", pe_b2)])
                    p.emit("vector", lambda v, h=h: v.tensor_mul(Ofin[:, h, :], Ofin[:, h, :],
                                                                 szal[:, h, :]),
                           waits=[("se", p.v("se"))], incs=[("ve", 1)])
                ve_of = p.v("ve")

                p.mute = mlvl < 7
                ve_s8ev = 0
                gp_s8 = p.v("gpd")
                for fo in range(4):
                    for h in range(4):
                        w8_ = []
                        if h == 0:
                            w8_.append(("ve", ve_of if fo == 0 else ve_s8ev))
                            if fo == 0:
                                w8_.append(("ldx", ldx_wo))
                        p.emit("tensor", lambda t, fo=fo, h=h: t.matmul(
                            ps_s8[:, :], Ofin[:, h, :], wo[:, h, fo * 512:(fo + 1) * 512],
                            start=(h == 0), stop=(h == 3)),
                               waits=w8_,
                               incs=[("pe", 1)] if h == 3 else [])
                    pe_s8 = p.v("pe")
                    p.emit("vector", lambda v, fo=fo: v.tensor_copy(s8stg[:, :], ps_s8[:, :]),
                           waits=[("pe", pe_s8), ("gpd", gp_s8)], incs=[("ve", 1)])
                    ve_s8ev = p.v("ve")
                    p.emit("gpsimd", lambda g, fo=fo: g.dma_start(
                        out=ccin[8:16, fo * 512:(fo + 1) * 512], in_=s8stg[:, :]),
                           waits=[("ve", ve_s8ev)], incs=[("gpd", 16)])
                    gp_s8 = p.v("gpd")
                ve_s8 = p.v("ve")
                if mlvl < 7:
                    p.mute = False
                    p.emit("vector", lambda v: v.memset(s8stg[:, :], 0.0), incs=[("ve", 1)])
                    ve_z8 = p.v("ve")
                    for fo in range(4):
                        p.emit("gpsimd", lambda g, fo=fo: g.dma_start(
                            out=ccin[8:16, fo * 512:(fo + 1) * 512], in_=s8stg[:, :]),
                               waits=[("ve", ve_z8)], incs=[("gpd", 16)])

        p.mute = False
        mini_end = {"ve": p.v("ve"), "se": p.v("se"), "pe": p.v("pe"),
                    "gpd": p.v("gpd")}

        with ExitStack() as ff:
            ef = ff.enter_context
            xin = ef(nc.sbuf_tensor("xin", [128, 3, D], F32))
            xntm = ef(nc.sbuf_tensor("xntm", [128, 2, D], BF16))
            bufX = ef(nc.sbuf_tensor("bufX", [128, NKD, 2048], BF16))
            bufH = ef(nc.sbuf_tensor("bufH", [128, NKD, 2048], BF16))
            wbuf = ef(nc.sbuf_tensor("wbuf", [128, 2, NKD, 512], BF16))
            ybuf = ef(nc.sbuf_tensor("ybuf", [128, 3, 512], BF16))
            rts = ef(nc.sbuf_tensor("rts", [128, 512], F32))
            rtf32 = ef(nc.sbuf_tensor("rtf32", [128, BT, 8], BF16))
            wgtb = ef(nc.sbuf_tensor("wgtb", [128, 2, BT, 8], BF16))
            wrs = ef(nc.sbuf_tensor("wrs", [128, NKD, 8], BF16))
            Pacc = ef(nc.sbuf_tensor("Pacc", [8, D], F32))
            scatacc = Pacc
            PTb = ef(nc.sbuf_tensor("PTb", [128, NKD, 8], BF16))
            nmean = ef(nc.sbuf_tensor("nmean", [128, 3], F32))
            vsum = ef(nc.sbuf_tensor("vsum", [128, 3], F32))
            svt = ef(nc.sbuf_tensor("svt", [128, 3], F32))
            rstd = ef(nc.sbuf_tensor("rstd", [128, 3], F32))
            Et = ef(nc.sbuf_tensor("Et", [128, 8], F32))
            E2t = ef(nc.sbuf_tensor("E2t", [128, 8], F32))
            msk1 = ef(nc.sbuf_tensor("msk1", [128, 8], F32))
            msk2 = ef(nc.sbuf_tensor("msk2", [128, 8], F32))
            rsc = ef(nc.sbuf_tensor("rsc", [128, 8], F32))

            ps_h = ef(nc.psum_tensor("ps_h", [128, 2, 512], F32))
            ps_trA = ef(nc.psum_tensor("ps_trA", [128, 512], BF16))
            ps_trB = ef(nc.psum_tensor("ps_trB", [128, 512], BF16))
            ps_rt = ef(nc.psum_tensor("ps_rt", [8, 512], F32))
            ps_rtt = ef(nc.psum_tensor("ps_rtt", [128, 4, 8], F32))
            ps_P = ef(nc.psum_tensor("ps_P", [8, 512], F32))

            p.mute = not use_ffn
            p.emit("gpsimd", lambda g: g.dma_start(out=wrs[:, :, :],
                                                   in_=wrT.rearrange("(k p) m -> p k m", p=128)),
                   waits=[("ve", mini_end["ve"]), ("se", mini_end["se"]),
                          ("pe", mini_end["pe"])],
                   incs=[("gpd", 16)])
            gp_wr = p.v("gpd")
            p.emit("vector", lambda v: v.memset(rts[:, :], 0.0),
                   waits=[("se", mini_end["se"]), ("pe", mini_end["pe"])])
            p.emit("vector", lambda v: v.memset(Pacc[:, :], 0.0), incs=[("ve", 1)])
            zpad_ve = p.v("ve")

            S = {
                "xin_store": [0, 0, 0],
                "xin_rd_ve": [0, 0, 0], "xin_rd_se": [0, 0, 0],
                "xntm_rd_pe": [0, 0], "stat_rd_se": [0, 0, 0],
                "ps_h_rd": [0, 0], "ps_h_rd_sem": ["se", "se"],
                "wbuf_rd_pe": [0, 0],
                "bufH_rd_pe": 0, "bufH_wr_se": 0,
                "ps_trX_rd": [("ve", 0), ("ve", 0)],
                "rt_ps_free_ve": zpad_ve, "rts_rd_pe": 0, "rtt_evac_ve": 0,
                "psP_free_ve": zpad_ve,
                "yb_se": [0, 0, 0], "ybuf_rd_pe": [0, 0, 0],
                "wgtb_wr_ve": 0, "ln_done": [0, 0, 0],
            }

            ld_marks = {}

            def emit_prep(b):
                base = b * BT

                def emit_load(u):
                    par = u % 3
                    gt = base + u
                    rows = (gt * 128, (gt + 1) * 128)
                    lw = [("st", S["xin_store"][par]), ("ve", S["xin_rd_ve"][par]),
                          ("se", S["xin_rd_se"][par])]
                    if b == 0 and u < 3:
                        lw += [("ve", mini_end["ve"]), ("se", mini_end["se"]),
                               ("pe", mini_end["pe"]), ("gpd", mini_end["gpd"])]
                    p.emit("sync", lambda s, par=par, rows=rows: s.dma_start(
                        out=xin[:, par, :], in_=x_sl[rows[0]:rows[1], :]),
                           waits=lw, incs=[("ldx", 16)])
                    ld_marks[(b, u)] = p.v("ldx")

                for tt in range(BT):
                    gt = base + tt
                    par = tt % 3
                    par2 = tt % 2
                    if tt == 0:
                        for u in range(3):
                            emit_load(u)
                    ld_t = ld_marks[(b, tt)]
                    if gt == 0:
                        p.emit("sync", lambda s, par=par: s.dma_start(
                            out=out[8:128, :], in_=xin[8:128, par, :]),
                               waits=[("ldx", ld_t)], incs=[("st", 16)])
                    else:
                        p.emit("sync", lambda s, par=par, gt=gt: s.dma_start(
                            out=out[gt * 128:(gt + 1) * 128, :], in_=xin[:, par, :]),
                               waits=[("ldx", ld_t)], incs=[("st", 16)])
                    S["xin_store"][par] = p.v("st")
                    p.emit("vector", lambda v, par=par: v.reduce_sum(
                        nmean[:, par:par + 1], xin[:, par, :], axis=AX.X, negate=True),
                           waits=[("ldx", ld_t), ("se", S["stat_rd_se"][par])])
                    p.emit("vector", lambda v, par=par: v.tensor_scalar_mul(
                        nmean[:, par:par + 1], nmean[:, par:par + 1], RD), incs=[("ve", 1)])
                    ve_nm = p.v("ve")
                    p.emit("scalar", lambda s, par=par, par2=par2: s.activation(
                        xntm[:, par2, :], xin[:, par, :], AF.Square, bias=nmean[:, par:par + 1],
                        accum_out=vsum[:, par:par + 1]),
                           waits=[("ve", ve_nm), ("pe", S["xntm_rd_pe"][par2])])
                    p.emit("scalar", lambda s, par=par: s.activation(
                        svt[:, par:par + 1], vsum[:, par:par + 1], AF.Sqrt, scale=RD, bias=cst[:, 0:1]),
                           incs=[("se", 1)])
                    se_sv = p.v("se")
                    S["xin_rd_se"][par] = se_sv
                    S["stat_rd_se"][par] = se_sv
                    p.emit("vector", lambda v, par=par: v.reciprocal(
                        rstd[:, par:par + 1], svt[:, par:par + 1]),
                           waits=[("se", se_sv)])
                    p.emit("vector", lambda v, par=par, par2=par2: v.tensor_scalar(
                        xntm[:, par2, :], xin[:, par, :], nmean[:, par:par + 1],
                        rstd[:, par:par + 1], ALU.add, ALU.mult),
                           incs=[("ve", 1)])
                    ve_norm = p.v("ve")
                    S["xin_rd_ve"][par] = ve_norm
                    for gidx in range(4):
                        bank = ps_trA if gidx % 2 == 0 else ps_trB
                        bwait = S["ps_trX_rd"][gidx % 2]
                        for j4 in range(4):
                            j = gidx * 4 + j4
                            p.emit("tensor", lambda t, par2=par2, j=j, j4=j4, bank=bank: t.transpose(
                                bank[:, j4 * 128:(j4 + 1) * 128], xntm[:, par2, j * 128:(j + 1) * 128],
                                identb[:, :]),
                                   waits=([("ve", ve_norm), (bwait[0], bwait[1])] if j4 == 0 else []),
                                   incs=[("pe", 1)] if j4 == 3 else [])
                        pe_tr = p.v("pe")
                        eng = "vector" if gidx % 2 == 0 else "scalar"
                        sem = "ve" if gidx % 2 == 0 else "se"
                        if eng == "vector":
                            p.emit(eng, lambda ee, tt=tt, gidx=gidx, bank=bank: ee.tensor_copy(
                                bufX[:, gidx * 4:(gidx + 1) * 4, tt * 128:(tt + 1) * 128],
                                bank[:, :].rearrange("p (j t) -> p j t", j=4)),
                                   waits=[("pe", pe_tr)], incs=[(sem, 1)])
                        else:
                            p.emit(eng, lambda ee, tt=tt, gidx=gidx, bank=bank: ee.copy(
                                bufX[:, gidx * 4:(gidx + 1) * 4, tt * 128:(tt + 1) * 128],
                                bank[:, :].rearrange("p (j t) -> p j t", j=4)),
                                   waits=[("pe", pe_tr)], incs=[(sem, 1)])
                        S["ps_trX_rd"][gidx % 2] = (sem, p.v(sem))
                    S["xntm_rd_pe"][par2] = p.v("pe")
                    if tt + 3 < BT:
                        emit_load(tt + 3)
                    yield
                S["ln_done"] = [p.v("ve"), p.v("se"), p.v("pe")]
                ln_done = S["ln_done"]
                for grp in range(4):
                    for ki in range(NKD):
                        p.emit("tensor", lambda t, grp=grp, ki=ki: t.matmul(
                            ps_rt[:, :], wrs[:, ki, :], bufX[:, ki, grp * 512:(grp + 1) * 512],
                            start=(ki == 0), stop=(ki == NKD - 1)),
                               waits=([("ve", ln_done[0]), ("se", ln_done[1]), ("gpd", gp_wr),
                                       ("ve", S["rt_ps_free_ve"])] if ki == 0 else []),
                               incs=[("pe", 1)] if ki == NKD - 1 else [])
                    pe_rt = p.v("pe")
                    p.emit("vector", lambda v: v.tensor_copy(rts[0:8, :], ps_rt[:, :]),
                           waits=[("pe", pe_rt), ("pe", S["rts_rd_pe"])], incs=[("ve", 1)])
                    ve_rts = p.v("ve")
                    S["rt_ps_free_ve"] = ve_rts
                    for t4 in range(4):
                        p.emit("tensor", lambda t, t4=t4: t.matmul(
                            ps_rtt[:, t4, :], rts[:, t4 * 128:(t4 + 1) * 128], ident[:, 0:8],
                            start=True, stop=True),
                               waits=([("ve", ve_rts), ("ve", S["rtt_evac_ve"])] if t4 == 0 else []),
                               incs=[("pe", 1)] if t4 == 3 else [])
                    pe_tt = p.v("pe")
                    S["rts_rd_pe"] = pe_tt
                    p.emit("vector", lambda v, grp=grp: v.tensor_copy(
                        rtf32[:, grp * 4:(grp + 1) * 4, :], ps_rtt[:, :, :]),
                           waits=[("pe", pe_tt)], incs=[("ve", 1)])
                    S["rtt_evac_ve"] = p.v("ve")
                    yield
                wp = b % 2
                for tt in range(BT):
                    p.emit("vector", lambda v, tt=tt: v.reduce_max(
                        rsc[:, 0:1], rtf32[:, tt, :], axis=AX.X, negate=True), incs=[("ve", 1)])
                    ve_m = p.v("ve")
                    p.emit("scalar", lambda s, tt=tt: s.activation(Et[:, :], rtf32[:, tt, :], AF.Exp,
                                                                   bias=rsc[:, 0:1]),
                           waits=[("ve", ve_m)], incs=[("se", 1)])
                    se_e = p.v("se")
                    p.emit("vector", lambda v: v.reduce_max(rsc[:, 1:2], Et[:, :], axis=AX.X),
                           waits=[("se", se_e)])
                    p.emit("vector", lambda v: v.tensor_scalar(msk1[:, :], Et[:, :], rsc[:, 1:2],
                                                               None, ALU.is_ge))
                    p.emit("vector", lambda v: v.scalar_tensor_tensor(E2t[:, :], msk1[:, :], -1e30,
                                                                      Et[:, :], ALU.mult, ALU.add))
                    p.emit("vector", lambda v: v.reduce_max(rsc[:, 2:3], E2t[:, :], axis=AX.X))
                    p.emit("vector", lambda v: v.tensor_scalar(msk2[:, :], E2t[:, :], rsc[:, 2:3],
                                                               None, ALU.is_ge))
                    p.emit("vector", lambda v: v.tensor_add(rsc[:, 3:4], rsc[:, 1:2], rsc[:, 2:3]))
                    p.emit("vector", lambda v: v.reciprocal(rsc[:, 3:4], rsc[:, 3:4]))
                    p.emit("vector", lambda v: v.tensor_mul(rsc[:, 4:5], rsc[:, 1:2], rsc[:, 3:4]))
                    p.emit("vector", lambda v: v.tensor_mul(rsc[:, 5:6], rsc[:, 2:3], rsc[:, 3:4]))
                    p.emit("vector", lambda v: v.tensor_scalar(E2t[:, :], msk1[:, :], rsc[:, 4:5],
                                                               None, ALU.mult))
                    p.emit("vector", lambda v, tt=tt, wp=wp: v.scalar_tensor_tensor(
                        wgtb[:, wp, tt, :], msk2[:, :], rsc[:, 5:6], E2t[:, :], ALU.mult, ALU.add),
                           incs=[("ve", 1)])
                    yield
                S["wgtb_wr_ve"] = p.v("ve")

            def emit_h1(b):
                ln_done = S["ln_done_h1"]
                for q in range(4):
                    wp = q % 2
                    p.emit("gpsimd", lambda g, q=q, wp=wp: g.dma_start(
                        out=wbuf[:, wp, 0:8, :],
                        in_=w1T[0:1024, q * 512:(q + 1) * 512].rearrange("(k p) m -> p k m", p=128)),
                           waits=[("pe", S["wbuf_rd_pe"][wp])], incs=[("gpd", 16)])
                    gp_w = p.v("gpd")
                    p.emit("gpsimd", lambda g, q=q, wp=wp: g.dma_start(
                        out=wbuf[:, wp, 8:16, :],
                        in_=w1T[1024:2048, q * 512:(q + 1) * 512].rearrange("(k p) m -> p k m", p=128)),
                           incs=[("gpd", 16)])
                    gp_w2 = p.v("gpd")
                    for mo in range(4):
                        fo = 4 * q + mo
                        for n4 in range(4):
                            hp = (mo * 4 + n4) % 2
                            w0 = [(S["ps_h_rd_sem"][hp], S["ps_h_rd"][hp])]
                            if mo == 0 and n4 == 0:
                                w0.append(("gpd", gp_w))
                                if q == 0:
                                    w0 += [("ve", ln_done[0]), ("se", ln_done[1])]
                            for ki in range(NKD):
                                wk_ = w0 if ki == 0 else []
                                if ki == 8 and mo == 0 and n4 == 0:
                                    wk_ = [("gpd", gp_w2)]
                                p.emit("tensor", lambda t, wp=wp, ki=ki, mo=mo, n4=n4, hp=hp: t.matmul(
                                    ps_h[:, hp, :],
                                    wbuf[:, wp, ki, mo * 128:(mo + 1) * 128],
                                    bufX[:, ki, n4 * 512:(n4 + 1) * 512],
                                    start=(ki == 0), stop=(ki == NKD - 1)),
                                       waits=wk_,
                                       incs=[("pe", 1)] if ki == NKD - 1 else [])
                            pe_mm = p.v("pe")
                            S["wbuf_rd_pe"][wp] = pe_mm
                            p.emit("scalar", lambda s, fo=fo, n4=n4, hp=hp: s.activation(
                                bufH[:, fo, n4 * 512:(n4 + 1) * 512],
                                ps_h[:, hp, :], AF.Silu),
                                   waits=[("pe", pe_mm), ("pe", S["bufH_rd_pe"])], incs=[("se", 1)])
                            S["ps_h_rd"][hp] = p.v("se")
                            S["ps_h_rd_sem"][hp] = "se"
                S["bufH_wr_se"] = p.v("se")

            def emit_h2(b, gen=None):
                wpb = b % 2
                wgtb_ve_b = S["wgtb_wr_ve"]
                for q in range(4):
                    wp = q % 2
                    p.emit("gpsimd", lambda g, q=q, wp=wp: g.dma_start(
                        out=wbuf[:, wp, 0:8, :],
                        in_=w3T[0:1024, q * 512:(q + 1) * 512].rearrange("(k p) m -> p k m", p=128)),
                           waits=[("pe", S["wbuf_rd_pe"][wp])], incs=[("gpd", 16)])
                    gp_w = p.v("gpd")
                    p.emit("gpsimd", lambda g, q=q, wp=wp: g.dma_start(
                        out=wbuf[:, wp, 8:16, :],
                        in_=w3T[1024:2048, q * 512:(q + 1) * 512].rearrange("(k p) m -> p k m", p=128)),
                           incs=[("gpd", 16)])
                    gp_w2 = p.v("gpd")

                    def emit_pmm(t_, q=q):
                        w = [("se", S["yb_se"][t_ % 3])]
                        if t_ == 0:
                            w += [("ve", S["psP_free_ve"]), ("ve", wgtb_ve_b)]
                        p.emit("tensor", lambda t, t_=t_: t.matmul(
                            ps_P[:, :], wgtb[:, wpb, t_, :], ybuf[:, t_ % 3, :],
                            start=(t_ == 0), stop=(t_ == BT - 1)),
                               waits=w, incs=[("pe", 1)])
                        S["ybuf_rd_pe"][t_ % 3] = p.v("pe")

                    for tt in range(BT):
                        hp = tt % 2
                        w0 = [(S["ps_h_rd_sem"][hp], S["ps_h_rd"][hp])]
                        if tt == 0:
                            w0.append(("gpd", gp_w))
                            if q == 0:
                                w0.append(("se", S["bufH_wr_se"]))
                        for ki in range(NKD):
                            wk_ = w0 if ki == 0 else []
                            if ki == 8 and tt == 0:
                                wk_ = [("gpd", gp_w2)]
                            p.emit("tensor", lambda t, wp=wp, ki=ki, tt=tt, hp=hp: t.matmul(
                                ps_h[:, hp, :], bufH[:, ki, tt * 128:(tt + 1) * 128],
                                wbuf[:, wp, ki, :],
                                start=(ki == 0), stop=(ki == NKD - 1)),
                                   waits=wk_,
                                   incs=[("pe", 1)] if ki == NKD - 1 else [])
                        pe_mm = p.v("pe")
                        S["wbuf_rd_pe"][wp] = pe_mm
                        S["bufH_rd_pe"] = pe_mm
                        p.emit("scalar", lambda s, tt=tt, hp=hp: s.activation(
                            ybuf[:, tt % 3, :], ps_h[:, hp, :], AF.Silu),
                               waits=[("pe", pe_mm), ("pe", S["ybuf_rd_pe"][tt % 3])], incs=[("se", 1)])
                        S["ps_h_rd"][hp] = p.v("se")
                        S["ps_h_rd_sem"][hp] = "se"
                        S["yb_se"][tt % 3] = p.v("se")
                        if gen is not None:
                            next(gen, None)
                        if tt >= 2:
                            emit_pmm(tt - 2)
                    emit_pmm(BT - 2)
                    emit_pmm(BT - 1)
                    pe_pl = p.v("pe")
                    p.emit("vector", lambda v, q=q: v.tensor_add(
                        Pacc[:, q * 512:(q + 1) * 512], Pacc[:, q * 512:(q + 1) * 512],
                        ps_P[:, :]),
                           waits=[("pe", pe_pl)], incs=[("ve", 1)])
                    S["psP_free_ve"] = p.v("ve")
                if gen is not None:
                    for _ in gen:
                        pass

            for _ in emit_prep(0):
                pass
            S["ln_done_h1"] = S["ln_done"]
            for b in range(NBLK):
                emit_h1(b)
                emit_h2(b, gen=emit_prep(b + 1) if b + 1 < NBLK else None)
                S["ln_done_h1"] = S["ln_done"]

            for g in range(4):
                p.emit("vector", lambda v, g=g: v.tensor_copy(
                    rts[0:8, :], Pacc[:, g * 512:(g + 1) * 512]),
                       waits=[("pe", S["rts_rd_pe"])], incs=[("ve", 1)])
                ve_stg = p.v("ve")
                for j in range(4):
                    p.emit("tensor", lambda t, j=j: t.matmul(
                        ps_rtt[:, j, :], rts[:, j * 128:(j + 1) * 128], ident[:, 0:8],
                        start=True, stop=True),
                           waits=([("ve", ve_stg), ("ve", S["rtt_evac_ve"])] if j == 0 else []),
                           incs=[("pe", 1)] if j == 3 else [])
                pe_t = p.v("pe")
                S["rts_rd_pe"] = pe_t
                p.emit("vector", lambda v, g=g: v.tensor_copy(
                    PTb[:, g * 4:(g + 1) * 4, :], ps_rtt[:, :, :]),
                       waits=[("pe", pe_t)], incs=[("ve", 1)])
                S["rtt_evac_ve"] = p.v("ve")
            ptb_ve = p.v("ve")
            sc_ps_free = S["psP_free_ve"]
            for q in range(4):
                wp = q % 2
                p.emit("gpsimd", lambda g, q=q, wp=wp: g.dma_start(
                    out=wbuf[:, wp, 0:8, :],
                    in_=w2T[0:1024, q * 512:(q + 1) * 512].rearrange("(k p) m -> p k m", p=128)),
                       waits=[("pe", S["wbuf_rd_pe"][wp])], incs=[("gpd", 16)])
                gp_w = p.v("gpd")
                p.emit("gpsimd", lambda g, q=q, wp=wp: g.dma_start(
                    out=wbuf[:, wp, 8:16, :],
                    in_=w2T[1024:2048, q * 512:(q + 1) * 512].rearrange("(k p) m -> p k m", p=128)),
                       incs=[("gpd", 16)])
                gp_w2 = p.v("gpd")
                for ki in range(NKD):
                    w0 = []
                    if ki == 0:
                        w0 = [("gpd", gp_w), ("ve", ptb_ve), ("ve", sc_ps_free)]
                    if ki == 8:
                        w0 = [("gpd", gp_w2)]
                    p.emit("tensor", lambda t, wp=wp, ki=ki: t.matmul(
                        ps_P[:, :], PTb[:, ki, :], wbuf[:, wp, ki, :],
                        start=(ki == 0), stop=(ki == NKD - 1)),
                           waits=w0, incs=[("pe", 1)] if ki == NKD - 1 else [])
                pe_sc = p.v("pe")
                S["wbuf_rd_pe"][wp] = pe_sc
                p.emit("vector", lambda v, q=q: v.tensor_copy(
                    scatacc[:, q * 512:(q + 1) * 512], ps_P[:, :]),
                       waits=[("pe", pe_sc)], incs=[("ve", 1)])
                sc_ps_free = p.v("ve")

            p.mute = False
            if not use_ffn:
                p.emit("vector", lambda v: v.memset(scatacc[:, :], 0.0), incs=[("ve", 1)])
            scat_final_ve = p.v("ve")

            cfin = xin[0:8, 0, :]
            x8l = xin[0:8, 1, :]
            s8rb = xin[0:8, 2, :]
            p.emit("sync", lambda s: s.dma_start(out=x8l, in_=x_sl[0:8, :]),
                   waits=[("st", S["xin_store"][1]), ("ve", S["xin_rd_ve"][1]),
                          ("se", S["xin_rd_se"][1])],
                   incs=[("ldx", 16)])
            ld_x8l = p.v("ldx")
            if dbg:
                p.emit("sync", lambda s: s.dma_start(out=dbg1, in_=scatacc[:, :]),
                       waits=[("ve", scat_final_ve)], incs=[("st", 16)])
            p.emit("gpsimd", lambda g: g.dma_start(out=ccin[0:8, :], in_=scatacc[:, :]),
                   waits=[("ve", scat_final_ve)], incs=[("gpd", 16)])
            gp_cc = p.v("gpd")
            if use_coll:
                p.emit("gpsimd", lambda g: g.collective_compute(
                    "AllReduce", ALU.add, replica_groups=[list(range(NCORE))],
                    ins=[ccin.ap().opt()], outs=[ccout.ap().opt()]),
                       waits=[("gpd", gp_cc)], incs=[("cc", 1)])
            else:
                p.emit("gpsimd", lambda g: g.dma_start(out=ccout[:, :], in_=ccin[:, :]),
                       waits=[("gpd", gp_cc)], incs=[("cc", 16)])
            p.emit("gpsimd", lambda g: g.dma_start(out=scatacc[:, :], in_=ccout[0:8, :]),
                   waits=[("cc", 1)], incs=[("gpd", 16)])
            p.emit("gpsimd", lambda g: g.dma_start(out=s8rb, in_=ccout[8:16, :]),
                   waits=[("st", S["xin_store"][2]), ("ve", S["xin_rd_ve"][2]),
                          ("se", S["xin_rd_se"][2])],
                   incs=[("gpd", 16)])
            gp_cr = p.v("gpd")
            if dbg:
                p.emit("sync", lambda s: s.dma_start(out=dbg3, in_=ccout[:, :]),
                       waits=[("cc", 1), ("gpd", gp_cr)], incs=[("st", 16)])
            p.emit("scalar", lambda s: s.activation(cfin, scatacc[:, :], AF.Silu),
                   waits=[("gpd", gp_cr), ("st", S["xin_store"][0]),
                          ("ve", S["xin_rd_ve"][0]),
                          ("se", S["xin_rd_se"][0])], incs=[("se", 1)])
            se_g8 = p.v("se")
            p.emit("vector", lambda v: v.tensor_mul(cfin, cfin, s8rb),
                   waits=[("se", se_g8), ("gpd", gp_cr)])
            p.emit("vector", lambda v: v.tensor_scalar(cfin, cfin, mask8[:, :], None,
                                                       ALU.mult),
                   waits=[("ldx", const_ld)])
            p.emit("vector", lambda v: v.tensor_add(cfin, cfin, x8l),
                   waits=[("ldx", ld_x8l)], incs=[("ve", 1)])
            ve_o8 = p.v("ve")
            p.emit("sync", lambda s: s.dma_start(out=out[0:8, :], in_=cfin),
                   waits=[("ve", ve_o8)], incs=[("st", 16)])
            p.emit("sync", lambda s: s.wait_ge(p.hand["st"], p.v("st")))


            block = root.enter_context(nc.Block())
            p.finalize(block)
    return nc


_NC_CACHE = None


def _get_nc():
    global _NC_CACHE
    if _NC_CACHE is None:
        _NC_CACHE = build_program()
    return _NC_CACHE


def prep_in_maps(L0_attn_hidden_states, L0_attn_W_qkv, L0_attn_W_z, L0_attn_W_a,
                 L0_attn_W_b, L0_attn_conv_w, L0_attn_A_log, L0_attn_dt_bias,
                 L0_attn_norm_w, L0_attn_W_out, L0_ffn_w1, L0_ffn_w3, L0_ffn_w2,
                 L0_ffn_wr):
    f32 = np.float32
    x = np.ascontiguousarray(np.asarray(L0_attn_hidden_states, dtype=f32)).reshape(TOK, D)
    Wqkv = np.asarray(L0_attn_W_qkv, dtype=f32)
    Wz = np.asarray(L0_attn_W_z, dtype=f32)
    Wa = np.asarray(L0_attn_W_a, dtype=f32)
    Wb = np.asarray(L0_attn_W_b, dtype=f32)
    cwf = np.asarray(L0_attn_conv_w, dtype=f32)
    Alog = np.asarray(L0_attn_A_log, dtype=f32)
    dtb = np.asarray(L0_attn_dt_bias, dtype=f32)
    nw = np.asarray(L0_attn_norm_w, dtype=f32)
    Wout = np.asarray(L0_attn_W_out, dtype=f32)
    w1 = np.asarray(L0_ffn_w1, dtype=f32)
    w3 = np.asarray(L0_ffn_w3, dtype=f32)
    w2 = np.asarray(L0_ffn_w2, dtype=f32)
    wr = np.asarray(L0_ffn_wr, dtype=f32)

    w1T = np.ascontiguousarray(w1.T)
    w3T = np.ascontiguousarray(w3.T)
    w2T = np.ascontiguousarray(w2.T)
    wrT = np.ascontiguousarray(wr.T)
    WoutT = np.ascontiguousarray(Wout.T)
    ones = np.ones((128, 128), f32)
    ident = np.eye(128, dtype=f32)
    identb = np.eye(128).astype(ml_dtypes.bfloat16)
    x8 = np.ascontiguousarray(x[0:8, :])

    in_maps = []
    for c in range(NCORE):
        kh0 = 2 * c * 128
        vh0 = 4 * c * 128
        in_maps.append({
            "x_sl": np.ascontiguousarray(x[c * TOKC:(c + 1) * TOKC]),
            "x8": x8,
            "w1T": w1T, "w3T": w3T, "w2T": w2T, "wrT": wrT,
            "wqT": np.ascontiguousarray(Wqkv[kh0:kh0 + 256, :].T),
            "wkT": np.ascontiguousarray(Wqkv[KEY + kh0:KEY + kh0 + 256, :].T),
            "wvT": np.ascontiguousarray(Wqkv[2 * KEY + vh0:2 * KEY + vh0 + 512, :].T),
            "wzT": np.ascontiguousarray(Wz[vh0:vh0 + 512, :].T),
            "wabT": np.ascontiguousarray(
                np.concatenate([Wa[4 * c:4 * c + 4], Wb[4 * c:4 * c + 4]], 0).T),
            "woT": np.ascontiguousarray(WoutT[vh0:vh0 + 512, :]),
            "cw": np.ascontiguousarray(np.concatenate(
                [cwf[kh0:kh0 + 256], cwf[KEY + kh0:KEY + kh0 + 256],
                 cwf[2 * KEY + vh0:2 * KEY + vh0 + 512]], 0)),
            "alog": np.ascontiguousarray(Alog[4 * c:4 * c + 4].reshape(4, 1)),
            "dtb": np.ascontiguousarray(dtb[4 * c:4 * c + 4].reshape(4, 1)),
            "normw": np.ascontiguousarray(nw.reshape(128, 1)),
            "mask8": np.full((8, 1), 1.0 if c == 0 else 0.0, f32),
            "ones": ones, "ident": ident, "identb": identb,
        })
    return in_maps


def kernel(**inputs):
    in_maps = prep_in_maps(**inputs)
    nc = _get_nc()
    res = run_bass_kernel_spmd(nc, in_maps, core_ids=list(range(NCORE)))
    outp = np.concatenate([res.results[c]["out"] for c in range(NCORE)], axis=0)
    return outp.reshape(B, S, D)


